# version 08
# baseline (speedup 1.0000x reference)
import numpy as np
import ml_dtypes

import concourse.bass as bass
import concourse.mybir as mybir
from concourse.bass_utils import run_bass_kernel_spmd
from concourse.dve_spec import Spec, Src0, Src1, C0, C1, Zero, One, maxx, minn, lower
from concourse.dve_op import DveOp
import concourse.dve_ops as dve_ops

F32 = mybir.dt.float32
BF16 = mybir.dt.bfloat16
AX = mybir.AxisListType
OP = mybir.AluOpType
ACT = mybir.ActivationFunctionType

C_VELT = float(2 ** 22 + 1)
C_MAGIC = float(0.75 * 2 ** 23)
_GRID = np.array([0.0, 0.5, 1.0, 1.5, 2.0, 3.0, 4.0, 6.0])
_MID = np.array([0.25, 0.75, 1.25, 1.75, 2.5, 3.5, 5.0])


def _fp4_ref(in0, in1, s0, s1, imm2, part):
    f = np.float64 if in0.dtype == np.float64 else np.float32
    xn = in0.astype(f) * in1.astype(f)
    if f is np.float64:
        idx = np.searchsorted(_MID, np.abs(xn).ravel()).reshape(np.shape(xn))
        r = _GRID[idx] * np.sign(xn)
    else:
        xn = xn.astype(np.float32)
        c = (xn * np.float32(C_VELT)).astype(np.float32)
        d = (c - xn).astype(np.float32)
        hi = (c - d).astype(np.float32)
        l = ((xn + np.float32(C_MAGIC)) - np.float32(C_MAGIC)).astype(np.float32)
        r = (hi - np.clip(hi, -2, 2)) + np.clip(l, -2, 2)
    out = (r - np.clip(r, -2.0, 2.0)) if part == "ovf" else np.clip(r, -2.0, 2.0)
    return out.astype(f)


def _mk_bodyA():
    xn = Src0 * Src1
    c = xn * C1
    d = c - xn
    hi = c - d
    m = minn(maxx(hi, Zero - (One + One)), One + One)
    return hi - m


def _mk_bodyB():
    xn = Src0 * Src1
    l = (xn + C0) - C0
    return minn(maxx(l, Zero - (One + One)), One + One)


def _mk_bodyS():
    return maxx(Src0, C0) * C1


def _scale6_ref(in0, in1, s0, s1, imm2):
    if in0.dtype == np.float64:
        return np.maximum(in0, np.float64(np.float32(6e-12))) / 6.0
    return (np.maximum(in0.astype(np.float32), np.float32(s0)) *
            np.float32(s1)).astype(np.float32)


def _register_fp4_ops():
    have = {op.name: op for op in dve_ops.OPS}
    if "FP4OVF" in have:
        return have["FP4OVF"], have["FP4LOW"], have["FP4SCALE"]
    shaA = {v: lower(Spec(body=_mk_bodyA()), ver=v, perf_en=False, name="FP4OVF").sha(v)
            for v in ("v3", "v4")}
    shaB = {v: lower(Spec(body=_mk_bodyB()), ver=v, perf_en=False, name="FP4LOW").sha(v)
            for v in ("v3", "v4")}
    opA = DveOp("FP4OVF", Spec(body=_mk_bodyA()), subdim=False, uops_sha=shaA,
                reference=lambda i0, i1, s0, s1, imm2: _fp4_ref(i0, i1, s0, s1, imm2, "ovf"))
    opB = DveOp("FP4LOW", Spec(body=_mk_bodyB()), subdim=False, uops_sha=shaB,
                reference=lambda i0, i1, s0, s1, imm2: _fp4_ref(i0, i1, s0, s1, imm2, "low"))
    shaS = {v: lower(Spec(body=_mk_bodyS()), ver=v, perf_en=False, name="FP4SCALE").sha(v)
            for v in ("v3", "v4")}
    opS = DveOp("FP4SCALE", Spec(body=_mk_bodyS()), subdim=False, uops_sha=shaS,
                reference=_scale6_ref)
    dve_ops.OPS.append(opA)
    dve_ops.OPS.append(opB)
    dve_ops.OPS.append(opS)
    dve_ops._SUB_OPCODE_FOR_NAME["FP4OVF"] = 31
    dve_ops._SUB_OPCODE_FOR_NAME["FP4LOW"] = 30
    dve_ops._SUB_OPCODE_FOR_NAME["FP4SCALE"] = 29
    for o in (opA, opB, opS):
        dve_ops.CUSTOM_DVE_SPECS[o.name] = o.spec
        dve_ops.CUSTOM_DVE_REFS[o.name] = o.ref
    return opA, opB, opS


OPA, OPB, OPS6 = _register_fp4_ops()

NSEQ = 4
STOP = 'Z'
S = 2048
D = 1024
NH = 16
I = 2816
NT = S // 128
NHP = NH // 2
NG = S // 64
AM_FLOOR = float(np.float32(6e-12))
FLOOR12 = float(np.float32(1e-12))
C6 = float(np.float32(1.0) / np.float32(6.0))


class Ctx:

    def __init__(self, active, engine, sems):
        self.a = active
        self.eng = engine
        self.sem = sems
        self.c = {k: 0 for k in sems}

    def on(self, eng, fn, inc=None, n=1):
        if self.a == eng:
            inst = fn(self.eng)
            if inc is not None:
                inst.then_inc(self.sem[inc], n)
        if inc is not None:
            self.c[inc] += n
            return self.c[inc]
        return None

    def wait(self, eng, sem, val):
        if val and val > 0 and self.a == eng:
            self.eng.wait_ge(self.sem[sem], val)

    def waitd(self, eng, sem):
        self.wait(eng, sem, self.c[sem])


def build_program():
    nc = bass.Bass("TRN2", target_bir_lowering=False, debug=False)

    x_in = nc.dram_tensor("x", [NSEQ * S, D], F32, kind="ExternalInput").ap()
    wq_in = nc.dram_tensor("wq", [D, D], F32, kind="ExternalInput").ap()
    wk_in = nc.dram_tensor("wk", [D, D], F32, kind="ExternalInput").ap()
    wv_in = nc.dram_tensor("wv", [D, D], F32, kind="ExternalInput").ap()
    wo_in = nc.dram_tensor("wo", [D, D], F32, kind="ExternalInput").ap()
    bo_in = nc.dram_tensor("bo", [1, D], F32, kind="ExternalInput").ap()
    wg_in = nc.dram_tensor("wg", [I, D], F32, kind="ExternalInput").ap()
    wu_in = nc.dram_tensor("wu", [I, D], F32, kind="ExternalInput").ap()
    wd_in = nc.dram_tensor("wd", [D, I], F32, kind="ExternalInput").ap()
    ident_in = nc.dram_tensor("ident", [128, 128], F32, kind="ExternalInput").ap()
    ones_in = nc.dram_tensor("onesb", [128, 128], BF16, kind="ExternalInput").ap()
    y_out = nc.dram_tensor("y", [NSEQ * S, D], F32, kind="ExternalOutput").ap()

    wqb = nc.dram_tensor("wqb", [D, D], BF16, kind="Internal").ap()
    wkb = nc.dram_tensor("wkb", [D, D], BF16, kind="Internal").ap()
    wvb = nc.dram_tensor("wvb", [D, D], BF16, kind="Internal").ap()
    wob = nc.dram_tensor("wob", [D, D], BF16, kind="Internal").ap()
    wgqb = nc.dram_tensor("wgqb", [D, I], BF16, kind="Internal").ap()
    wuqb = nc.dram_tensor("wuqb", [D, I], BF16, kind="Internal").ap()
    wdqb = nc.dram_tensor("wdqb", [I, D], BF16, kind="Internal").ap()
    xmid_hbm = nc.dram_tensor("xmidh", [S, D], F32, kind="Internal").ap()

    abf = nc.alloc_sbuf_tensor("abf", [128, 88 * 1024], BF16)
    af = nc.alloc_sbuf_tensor("af", [128, 7424], F32)
    ident = nc.alloc_sbuf_tensor("identsb", [128, 128], F32)
    onesb = nc.alloc_sbuf_tensor("onessb", [128, 128], BF16)
    bob = nc.alloc_sbuf_tensor("bobsb", [1, 1024], BF16)
    AB, AF = abf.ap(), af.ap()

    def bfv(off, shape):
        n = int(np.prod(shape))
        v = AB[:, off:off + n]
        if len(shape) > 1:
            v = v.rearrange("p (a b) -> p a b", b=shape[-1])
        return v

    def fv(off, shape):
        n = int(np.prod(shape))
        v = AF[:, off:off + n]
        if len(shape) > 1:
            v = v.rearrange("p (a b) -> p a b", b=shape[-1])
        return v

    K_ = 1024
    xn_fm = bfv(0, [8, S])
    xq_fm = xn_fm
    v_sb = bfv(16 * K_, [NT, 1024])
    wd_sb = bfv(16 * K_, [22, 1024])
    q_sb = [bfv(32 * K_ + i * 4 * K_, [S]) for i in range(2)]
    k_sb = [bfv(40 * K_ + i * 4 * K_, [S]) for i in range(2)]
    o_sb = bfv(48 * K_, [8, S])
    wg_sb = bfv(40 * K_, [8, I])
    et = [bfv(64 * K_ + i * K_, [1024]) for i in range(2)]
    wq_sb = bfv(66 * K_, [8, 1024])
    wo_sb = wq_sb
    wk_sb = bfv(74 * K_, [8, 1024])
    wu_sb = bfv(62 * K_ + 512, [8, I])
    stg_b = [bfv(85 * K_ + i * K_, [1024]) for i in range(2)]
    qt_ovf = bfv(85 * K_, [1024])
    qt_low = bfv(86 * K_, [1024])
    qt_shb = bfv(87 * K_, [176])
    hq_fm = bfv(38 * K_, [1024])
    wfm = bfv(24 * K_, [8, I])
    wfm_d = bfv(24 * K_, [22, 1024])
    sq_ovf = bfv(48 * K_, [I])
    sq_low = bfv(51 * K_, [I])
    sq_shb = bfv(54 * K_, [176])

    xf = [fv(i * K_, [1024]) for i in range(2)]
    xf2 = [fv(2 * K_ + i * K_, [1024]) for i in range(2)]
    xn2f = fv(4 * K_, [1024])
    lrec = fv(5 * K_, [512])
    osum = fv(5 * K_ + 512, [512])
    sgf = fv(6 * K_, [512])
    wdstg = [fv(0, [I]), fv(3 * K_, [I])]
    bnb = fv(6656, [12])
    mv = fv(6672, [2])
    lnv = fv(6676, [1])
    rstd = fv(6680, [1])
    am_t = fv(6688, [176])
    epsb = fv(7232, [1])
    sh_t = fv(6864, [176])
    rc_t = fv(7040, [176])

    ps_a = nc.alloc_psum_tensor("ps_a", [128, 1024], F32)
    ps_b = nc.alloc_psum_tensor("ps_b", [128, 1024], F32)
    ps_o1 = nc.alloc_psum_tensor("ps_o1", [128, 512], F32)
    ps_o2 = nc.alloc_psum_tensor("ps_o2", [128, 512], F32)
    ps_l1 = nc.alloc_psum_tensor("ps_l1", [128, 512], F32)
    ps_l2 = nc.alloc_psum_tensor("ps_l2", [128, 512], F32)
    PSA, PSB = ps_a.ap(), ps_b.ap()
    PSO1, PSO2, PSL1, PSL2 = ps_o1.ap(), ps_o2.ap(), ps_l1.ap(), ps_l2.ap()
    IDT = ident.ap()
    ONE = onesb.ap()
    BOB = bob.ap()

    def emit(ctx: Ctx):
        c = ctx.c

        def dma_y(dst, src, sem='C'):
            return ctx.on('y', lambda e: e.dma_start(out=dst, in_=src), inc=sem, n=16)

        def dma_w(dst, src):
            return ctx.on('s', lambda e: e.dma_start(out=dst, in_=src), inc='W', n=16)

        def dma_g(dst, src):
            return ctx.on('g', lambda e: e.dma_start(out=dst, in_=src), inc='G', n=16)

        def vop(fn, inc=True):
            return ctx.on('v', fn, inc='V' if inc else None, n=1)

        def sop(fn, inc=True):
            return ctx.on('s', fn, inc='S' if inc else None, n=1)

        def top(fn, inc=True):
            return ctx.on('t', fn, inc='T' if inc else None, n=1)

        def vdrain():
            ctx.wait('v', 'V', c['V'])

        def ttranspose(out_ap, in_ap, inc=False):
            return top(lambda e: e.matmul(out_ap, in_ap, IDT[:, :],
                                          is_transpose=True, skip_group_check=True),
                       inc=inc)

        class Quant:

            def __init__(self, dst, src_f32, width, scr=None, war=0):
                self.dst = dst
                self.src = src_f32[:, 0:width].rearrange("p (b k) -> p b k", k=16)
                self.w = width
                self.nb = width // 16
                self.scr = scr if scr is not None else (qt_ovf, qt_low, qt_shb)
                self.war = war

            def s_reduce(self, src_done=0):
                ctx.wait('v', 'V', max(self.war, src_done))
                self.c_red = vop(lambda e: e.tensor_reduce(
                    am_t[:, 0:self.nb], self.src, axis=AX.X, op=OP.max,
                    apply_absolute_value=True))

            def s_scale(self):
                ctx.wait('v', 'V', self.c_red)
                self.c_sc = vop(lambda e: e._custom_dve(
                    OPS6, out=sh_t[:, 0:self.nb], in0=am_t[:, 0:self.nb],
                    s0=AM_FLOOR, s1=C6))

            def s_recip(self):
                ctx.wait('v', 'V', self.c_sc)
                vop(lambda e: e.tensor_copy(self.scr[2][:, 0:self.nb],
                                            sh_t[:, 0:self.nb]), inc=False)
                self.c_rc = vop(lambda e: e.reciprocal_approx_fast(
                    rc_t[:, 0:self.nb], sh_t[:, 0:self.nb]))

            def s_ops(self):
                ctx.wait('v', 'V', self.c_rc)
                rcb = rc_t[:, 0:self.nb].broadcast_to([128, self.nb, 16])
                vop(lambda e, r_=rcb: e._custom_dve(
                    OPA, out=self.scr[0][:, 0:self.w].rearrange("p (b k) -> p b k", k=16),
                    in0=self.src, in1=r_, s0=0.0, s1=C_VELT), inc=False)
                self.c_ops = vop(lambda e, r_=rcb: e._custom_dve(
                    OPB, out=self.scr[1][:, 0:self.w].rearrange("p (b k) -> p b k", k=16),
                    in0=self.src, in1=r_, s0=C_MAGIC, s1=0.0))

            def s_add(self):
                ctx.wait('v', 'V', self.c_ops)
                self.c_add = vop(lambda e: e.tensor_tensor(
                    self.scr[0][:, 0:self.w], self.scr[0][:, 0:self.w],
                    self.scr[1][:, 0:self.w], OP.add))

            def s_mult(self, dst_war=0):
                ctx.wait('v', 'V', self.c_add)
                if dst_war:
                    ctx.wait('v', 'T', dst_war)
                shb = self.scr[2][:, 0:self.nb].broadcast_to([128, self.nb, 16])
                self.c_mul = vop(lambda e: e.tensor_tensor(
                    self.dst[:, 0:self.w].rearrange("p (b k) -> p b k", k=16),
                    self.scr[0][:, 0:self.w].rearrange("p (b k) -> p b k", k=16),
                    shb, OP.mult))
                return self.c_mul

        def quant_tile(dst_bf, src_f32, width, scr=None, war=0, src_done=0,
                       dst_war=0):
            q = Quant(dst_bf, src_f32, width, scr=scr, war=war)
            q.s_reduce(src_done)
            q.s_scale()
            q.s_recip()
            q.s_ops()
            q.s_add()
            return q.s_mult(dst_war)

        def ttranspose(out_ap, in_ap, inc=False):
            return top(lambda e: e.matmul(out_ap, in_ap, IDT[:, :],
                                          is_transpose=True, skip_group_check=True),
                       inc=inc)

        def quant_tile(dst_bf, src_f32, width, scr=None):
            t_ovf, t_low, t_shb = scr if scr is not None else (qt_ovf, qt_low, qt_shb)
            nb = width // 16
            sv = src_f32[:, 0:width].rearrange("p (b k) -> p b k", k=16)
            vdrain()
            vop(lambda e: e.tensor_reduce(am_t[:, 0:nb], sv, axis=AX.X, op=OP.max,
                                          apply_absolute_value=True))
            vdrain()
            vop(lambda e: e._custom_dve(OPS6, out=sh_t[:, 0:nb], in0=am_t[:, 0:nb],
                                        s0=AM_FLOOR, s1=C6))
            vdrain()
            vop(lambda e: e.tensor_copy(t_shb[:, 0:nb], sh_t[:, 0:nb]))
            vop(lambda e: e.reciprocal_approx_fast(rc_t[:, 0:nb], sh_t[:, 0:nb]))
            vdrain()
            rcb = rc_t[:, 0:nb].broadcast_to([128, nb, 16])
            shb = t_shb[:, 0:nb].broadcast_to([128, nb, 16])
            vop(lambda e: e._custom_dve(
                OPA, out=t_ovf[:, 0:width].rearrange("p (b k) -> p b k", k=16),
                in0=sv, in1=rcb, s0=0.0, s1=C_VELT))
            vop(lambda e: e._custom_dve(
                OPB, out=t_low[:, 0:width].rearrange("p (b k) -> p b k", k=16),
                in0=sv, in1=rcb, s0=C_MAGIC, s1=0.0))
            vdrain()
            vop(lambda e: e.tensor_tensor(t_ovf[:, 0:width], t_ovf[:, 0:width],
                                          t_low[:, 0:width], OP.add))
            vdrain()
            return vop(lambda e: e.tensor_tensor(
                dst_bf[:, 0:width].rearrange("p (b k) -> p b k", k=16),
                t_ovf[:, 0:width].rearrange("p (b k) -> p b k", k=16),
                shb, OP.mult))

        ctx.on('v', lambda e: e.memset(epsb[:, 0:1], 1e-5))
        dma_y(IDT[:, :], ident_in[:, :])
        dma_y(ONE[:, :], ones_in[:, :])
        cl_bo = dma_y(xf[0][0:1, :], bo_in[0:1, :])
        ctx.waitd('v', 'C')
        v_bo = vop(lambda e: e.tensor_copy(BOB[0:1, :], xf[0][0:1, :]))

        ctx.waitd('t', 'C')

        cast_v = {}

        def attn_cast_load(i):
            b = i % 2
            wi, ch = divmod(i, 8)
            src = (wq_in, wk_in, wv_in, wo_in)[wi]
            ctx.wait('y', 'V', cast_v[i - 2] if i >= 2 else v_bo)
            dma_y(xf[b][:, :], src[ch * 128:(ch + 1) * 128, :], sem='L%d' % b)

        def attn_cast_do(i):
            b = i % 2
            wi, ch = divmod(i, 8)
            dst = (wqb, wkb, wvb, wob)[wi]
            ctx.waitd('v', 'L%d' % b)
            if i >= 2:
                ctx.waitd('v', 'C')
            cast_v[i] = vop(lambda e, b_=b: e.tensor_copy(stg_b[b_][:, :],
                                                          xf[b_][:, :]))
            ctx.wait('y', 'V', cast_v[i])
            ctx.on('y', lambda e, d_=dst, c_=ch, b_=b: e.dma_start(
                out=d_[c_ * 128:(c_ + 1) * 128, :], in_=stg_b[b_][:, :]),
                inc='C', n=16)

        CH = []
        for wsel, (src_w, dstb) in enumerate(((wg_in, wgqb), (wu_in, wuqb),
                                              (wd_in, wdqb))):
            for chq in range(8 if wsel == 2 else 22):
                CH.append((wsel, chq, src_w, dstb))
        NCH = len(CH)
        scrA = (sq_ovf, sq_low, sq_shb)
        scrB = (bfv(0, [I]), bfv(3 * K_, [I]), bfv(6 * K_, [176]))

        def stg_of(ci):
            wsel = CH[ci][0]
            return ((wdstg[ci % 2], I) if wsel == 2 else (xf2[ci % 2], 1024))

        tr_chunk = {}
        ev_group = {}
        g_state = [0]
        qS = {}

        def load_ch(ci):
            wsel, chq, src_w, dstb = CH[ci]
            dst, wid = stg_of(ci)
            if ci - 2 in tr_chunk:
                ctx.wait('s', 'T', tr_chunk[ci - 2])
            if ci in (44, 45):
                ctx.wait('s', 'V', cast_v[31])
            ctx.on('s', lambda e, d_=dst, w_=wid, s_=src_w, q_=chq: e.dma_start(
                out=d_[:, 0:w_], in_=s_[q_ * 128:(q_ + 1) * 128, :]),
                inc='M%d' % (ci % 2), n=16)

        def tr_evac_ch(ci):
            wsel, chq, src_w, dstb = CH[ci]
            dst, wid = stg_of(ci)
            ncol = wid // 128
            fm = wfm_d if wsel == 2 else wfm
            nch_w = 8 if wsel == 2 else 22
            ctx.wait('t', 'V', qS[ci].c_mul)
            for b0 in range(0, ncol, 8):
                nb8 = min(8, ncol - b0)
                g = g_state[0]
                psu = [PSA, PSB][g % 2]
                if g - 2 in ev_group:
                    ctx.wait('t', 'S', ev_group[g - 2])
                for j in range(nb8):
                    bb = b0 + j
                    ttranspose(psu[:, j * 128:(j + 1) * 128],
                               dst[:, bb * 128:(bb + 1) * 128],
                               inc=(j == nb8 - 1))
                ctx.wait('s', 'T', c['T'])
                if b0 == 0 and chq == 0:
                    ctx.waitd('s', 'G')
                ctx.on('s', lambda e, p_=psu, n_=nb8, f_=fm, b_=b0, q_=chq:
                       e.activation(f_[:, b_:b_ + n_, q_ * 128:(q_ + 1) * 128],
                                    p_[:, 0:n_ * 128].rearrange(
                                        "p (a n) -> p a n", n=128),
                                    ACT.Copy),
                       inc='S', n=1)
                ev_group[g] = c['S']
                g_state[0] = g + 1
            tr_chunk[ci] = c['T']
            if chq == nch_w - 1:
                ctx.wait('g', 'S', c['S'])
                dma_g(dstb.rearrange("(a p) n -> p a n", p=128), fm[:, :, :])

        attn_cast_load(0)
        attn_cast_load(1)
        load_ch(0)
        load_ch(1)
        for ci in range(NCH):
            if ci >= 1:
                qS[ci - 1].s_add()
                qS[ci - 1].s_mult()
                tr_evac_ch(ci - 1)
                if ci + 1 < NCH:
                    load_ch(ci + 1)
            if ci < 32:
                attn_cast_do(ci)
            if ci + 2 <= 31:
                attn_cast_load(ci + 2)
            dst, wid = stg_of(ci)
            ctx.waitd('v', 'M%d' % (ci % 2))
            q = Quant(dst, dst, wid, scr=(scrA if ci % 2 == 0 else scrB))
            qS[ci] = q
            q.s_reduce(src_done=(qS[ci - 1].c_mul if ci >= 1 else 0))
            q.s_scale()
            q.s_recip()
            q.s_ops()
        qS[NCH - 1].s_add()
        qS[NCH - 1].s_mult()
        tr_evac_ch(NCH - 1)
        last_attnw_cast = cast_v[31]
        setup_done_G = c['G']
        setup_done_V = c['V']
        setup_done_T = c['T']
        setup_done_S = c['S']

        if STOP == 'S':
            ctx.on('g', lambda e: e.wait_ge(ctx.sem['G'], c['G']))
            return
        ctx.wait('v', 'T', setup_done_T)
        ctx.wait('t', 'S', setup_done_S)
        xf_guard = [('T', tr_chunk[50]), ('T', tr_chunk[50])]
        xf2_guard = [('T', tr_chunk[50]), ('T', tr_chunk[51])]
        prev_seq_V_end = 0

        for s_i in range(NSEQ):
            xbase = s_i * S
            ctx.waitd('s', 'G')
            ctx.waitd('s', 'C')
            wq_l = dma_w(wq_sb[:, :, :], wqb.rearrange("(a p) n -> p a n", p=128))
            wk_l = dma_w(wk_sb[:, :, :], wkb.rearrange("(a p) n -> p a n", p=128))
            wv_l = dma_w(v_sb[:, 0, 0:1024] if False else bfv(48 * K_, [8, 1024])[:, :, :],
                         wvb.rearrange("(a p) n -> p a n", p=128))
            wv_sb = bfv(48 * K_, [8, 1024])

            evacA = []
            for t in range(NT):
                b = t % 2
                g_eng, g_val = xf_guard[b]
                if g_eng == 'G':
                    ctx.waitd('y', 'G')
                else:
                    ctx.wait('y', g_eng, g_val)
                cl = dma_y(xf[b][:, :], x_in[xbase + t * 128: xbase + (t + 1) * 128, :],
                           sem='L%d' % b)
                ctx.waitd('v', 'L%d' % b)
                vop(lambda e, b_=b: e.bn_stats(bnb[:, 0:6], xf[b_][:, 0:512]))
                vop(lambda e, b_=b: e.bn_stats(bnb[:, 6:12], xf[b_][:, 512:1024]))
                vdrain()
                vbn = vop(lambda e: e.bn_aggr(mv[:, 0:2], bnb[:, 0:12]))
                ctx.wait('s', 'V', vbn)
                sop(lambda e: e.activation(lnv[:, 0:1], mv[:, 1:2], ACT.Ln, bias=epsb[:, 0:1]))
                ctx.wait('s', 'S', c['S'])
                srs = sop(lambda e: e.activation(rstd[:, 0:1], lnv[:, 0:1], ACT.Exp,
                                                 scale=-0.5))
                ctx.wait('v', 'S', srs)
                vap = vop(lambda e, b_=b: e.tensor_scalar(
                    xn2f[:, :], xf[b_][:, :], mv[:, 0:1], rstd[:, 0:1],
                    OP.subtract, OP.mult))
                xf_guard[b] = ('V', vap)
                ctx.wait('t', 'V', vap)
                if t >= 2:
                    ctx.wait('t', 'V', evacA[t - 2])
                elif s_i > 0:
                    ctx.wait('t', 'V', prev_seq_V_end)
                psT = [PSA, PSB][t % 2]
                for j in range(8):
                    ttranspose(psT[:, j * 128:(j + 1) * 128],
                               xn2f[:, j * 128:(j + 1) * 128], inc=(j == 7))
                ctx.wait('v', 'T', c['T'])
                ev = vop(lambda e, t_=t, p_=psT: e.tensor_copy(
                    xn_fm[:, :, t_ * 128:(t_ + 1) * 128],
                    p_[:, :].rearrange("p (a n) -> p a n", n=128)))
                evacA.append(ev)

            if STOP == 'A':
                continue
            ctx.waitd('t', 'W')
            vprod_ev = []
            for t in range(NT):
                psu = [PSA, PSB][t % 2]
                ctx.wait('t', 'V', evacA[-1])
                if t >= 2:
                    ctx.wait('t', 'V', vprod_ev[t - 2])
                for n in range(2):
                    for dc in range(8):
                        top(lambda e, p_=psu, n_=n, d_=dc, t_=t: e.matmul(
                            p_[:, n_ * 512:(n_ + 1) * 512],
                            xn_fm[:, d_, t_ * 128:(t_ + 1) * 128],
                            wv_sb[:, d_, n_ * 512:(n_ + 1) * 512],
                            start=(d_ == 0), stop=(d_ == 7)),
                            inc=(n == 1 and d_ == 7) if False else (n == 1 and dc == 7))
                ctx.wait('v', 'T', c['T'])
                vprod_ev.append(vop(lambda e, p_=psu, t_=t: e.tensor_copy(
                    v_sb[:, t_, :], p_[:, :])))

            if STOP == 'B':
                continue
            ctx.waitd('t', 'W')
            last_exp_bank = [0, 0]
            last_pv_et = [0, 0]
            ep_free_v = 0
            ep_psum_free = 0
            last_exp_all = 0
            for hp in range(NHP):
                pb = hp % 2
                for wi, (wsb, dsb, psu, par) in enumerate(
                        ((wq_sb, q_sb[pb], PSA, 0), (wk_sb, k_sb[pb], PSB, 1))):
                    ctx.wait('t', 'S', last_exp_bank[par])
                    if hp == 0:
                        ctx.wait('t', 'V', vprod_ev[-1])
                    ve_half = [0, 0]
                    for n in range(4):
                        if n >= 2:
                            ctx.wait('t', 'V', ve_half[n % 2])
                        for dc in range(8):
                            top(lambda e, p_=psu, w_=wsb, n_=n, d_=dc: e.matmul(
                                p_[:, (n_ % 2) * 512:((n_ % 2) + 1) * 512],
                                w_[:, d_, hp * 128:(hp + 1) * 128],
                                xn_fm[:, d_, n_ * 512:(n_ + 1) * 512],
                                start=(d_ == 0), stop=(d_ == 7)),
                                inc=(dc == 7))
                        ctx.wait('v', 'T', c['T'])
                        ve_half[n % 2] = vop(lambda e, p_=psu, d_=dsb, n_=n: e.tensor_copy(
                            d_[:, n_ * 512:(n_ + 1) * 512],
                            p_[:, (n_ % 2) * 512:((n_ % 2) + 1) * 512]))
                qk_ready = c['V']
                if STOP == 'C1':
                    continue

                for qc in range(4):
                    q0 = qc * 512
                    pv_args = {}

                    def emit_S_exp(up):
                        ge, go = 2 * up, 2 * up + 1
                        par = up % 2
                        sb = [PSA, PSB][par]
                        eb = et[par]
                        if up == 0:
                            ctx.wait('t', 'V', qk_ready)
                        ctx.wait('t', 'S', last_exp_bank[par])
                        top(lambda e, s_=sb: e.matmul(
                            s_[0:64, 0:512], k_sb[pb][0:64, ge * 64:(ge + 1) * 64],
                            q_sb[pb][0:64, q0:q0 + 512], start=True, stop=True,
                            skip_group_check=True), inc=False)
                        top(lambda e, s_=sb: e.matmul(
                            s_[64:128, 0:512], k_sb[pb][0:64, go * 64:(go + 1) * 64],
                            q_sb[pb][0:64, q0:q0 + 512], start=True, stop=True,
                            skip_group_check=True), inc=False)
                        top(lambda e, s_=sb: e.matmul(
                            s_[0:64, 512:1024], k_sb[pb][64:128, ge * 64:(ge + 1) * 64],
                            q_sb[pb][64:128, q0:q0 + 512], start=True, stop=True,
                            skip_group_check=True), inc=False)
                        tS = top(lambda e, s_=sb: e.matmul(
                            s_[64:128, 512:1024], k_sb[pb][64:128, go * 64:(go + 1) * 64],
                            q_sb[pb][64:128, q0:q0 + 512], start=True, stop=True,
                            skip_group_check=True))
                        ctx.wait('s', 'T', tS)
                        ctx.wait('s', 'T', last_pv_et[par])
                        se = sop(lambda e, s_=sb, b_=par: e.activation(
                            et[b_][:, :], s_[:, :], ACT.Exp, scale=0.125))
                        last_exp_bank[par] = se
                        pv_args[up] = (ge, go, par, se)
                        return se

                    def emit_PVl(up):
                        ge, go, par, se = pv_args[up]
                        eb = et[par]
                        ctx.wait('t', 'S', se)
                        if up == 0:
                            ctx.wait('t', 'V', ep_psum_free)
                        ste = (up == 0)
                        stp = (up == NG // 2 - 1)
                        vA_e = v_sb[0:64, ge // 2, hp * 128:hp * 128 + 64]
                        vB_e = v_sb[0:64, ge // 2, hp * 128 + 64:hp * 128 + 128]
                        vA_o = v_sb[64:128, go // 2, hp * 128:hp * 128 + 64]
                        vB_o = v_sb[64:128, go // 2, hp * 128 + 64:hp * 128 + 128]
                        top(lambda e, v_=vA_e, b_=eb: e.matmul(
                            PSO1[0:64, :], v_, b_[0:64, 0:512], start=ste, stop=stp,
                            skip_group_check=True), inc=False)
                        top(lambda e, v_=vB_e, b_=eb: e.matmul(
                            PSO1[64:128, :], v_, b_[0:64, 512:1024], start=ste, stop=stp,
                            skip_group_check=True), inc=False)
                        top(lambda e, b_=eb: e.matmul(
                            PSL1[0:64, :], ONE[64:128, 0:64], b_[64:128, 0:512],
                            start=ste, stop=stp, skip_group_check=True), inc=False)
                        top(lambda e, b_=eb: e.matmul(
                            PSL1[64:128, :], ONE[64:128, 0:64], b_[64:128, 512:1024],
                            start=ste, stop=stp, skip_group_check=True), inc=False)
                        top(lambda e, v_=vA_o, b_=eb: e.matmul(
                            PSO2[0:64, :], v_, b_[64:128, 0:512], start=ste, stop=stp,
                            skip_group_check=True), inc=False)
                        top(lambda e, v_=vB_o, b_=eb: e.matmul(
                            PSO2[64:128, :], v_, b_[64:128, 512:1024], start=ste, stop=stp,
                            skip_group_check=True), inc=False)
                        top(lambda e, b_=eb: e.matmul(
                            PSL2[0:64, :], ONE[0:64, 0:64], b_[0:64, 0:512],
                            start=ste, stop=stp, skip_group_check=True), inc=False)
                        tpv_ = top(lambda e, b_=eb: e.matmul(
                            PSL2[64:128, :], ONE[0:64, 0:64], b_[0:64, 512:1024],
                            start=ste, stop=stp, skip_group_check=True))
                        last_pv_et[par] = tpv_
                        return tpv_

                    for up in range(NG // 2):
                        emit_S_exp(up)
                        last_exp_all = pv_args[up][3]
                        if up >= 1:
                            emit_PVl(up - 1)
                    tpv = emit_PVl(NG // 2 - 1)
                    if STOP in ('C2', 'C3'):
                        continue
                    ctx.wait('v', 'T', tpv)
                    vdrain()
                    vop(lambda e: e.tensor_copy(sgf[:, :], PSL1[:, :]))
                    vop(lambda e: e.tensor_copy(osum[:, :], PSO1[:, :]))
                    vdrain()
                    vop(lambda e: e.tensor_tensor(sgf[:, :], PSL2[:, :], sgf[:, :],
                                                  OP.add))
                    vop(lambda e: e.tensor_tensor(osum[:, :], PSO2[:, :], osum[:, :],
                                                  OP.add))
                    ep_psum_free = c['V']
                    vdrain()
                    vop(lambda e: e.reciprocal_approx_fast(lrec[:, :], sgf[:, :]))
                    vdrain()
                    ep_free_v = vop(lambda e, h_=hp, q_=q0: e.tensor_tensor(
                        o_sb[:, h_, q_:q_ + 512], osum[:, :], lrec[:, :], OP.mult))

            wo_l = dma_w(wo_sb[:, :, :], wob.rearrange("(a p) n -> p a n", p=128))
            ctx.wait('t', 'W', wo_l)
            ctx.wait('t', 'V', ep_free_v)
            ctx.wait('t', 'S', last_exp_all)
            xq_evac = []
            xmid_store = {}
            vxm_hist = {}
            trD_T = {}
            qD = {}

            def emit_oprojD(t_):
                psu = [PSA, PSB][t_ % 2]
                if t_ >= 2:
                    ctx.wait('t', 'V', vxm_hist[t_ - 2])
                for n in range(2):
                    for ic in range(8):
                        top(lambda e, p_=psu, n_=n, i_=ic, tt_=t_: e.matmul(
                            p_[:, n_ * 512:(n_ + 1) * 512],
                            o_sb[:, i_, tt_ * 128:(tt_ + 1) * 128],
                            wo_sb[:, i_, n_ * 512:(n_ + 1) * 512],
                            start=(i_ == 0), stop=False), inc=False)
                    top(lambda e, p_=psu, n_=n: e.matmul(
                        p_[:, n_ * 512:(n_ + 1) * 512], ONE[0:1, 0:128],
                        BOB[0:1, n_ * 512:(n_ + 1) * 512],
                        start=False, stop=True), inc=(n == 1))
                return c['T']

            def emit_trD(t_):
                ctx.wait('t', 'V', qD[t_].c_mul)
                for j in range(8):
                    psl = [PSL1, PSL2][j // 4]
                    ttranspose(psl[:, (j % 4) * 128:((j % 4) + 1) * 128],
                               xf[t_ % 2][:, j * 128:(j + 1) * 128], inc=(j == 7))
                trD_T[t_] = c['T']

            def emit_evacD(t_):
                ctx.wait('v', 'T', trD_T[t_])
                vop(lambda e, t__=t_: e.tensor_copy(
                    xq_fm[:, 0:4, t__ * 128:(t__ + 1) * 128],
                    PSL1[:, :].rearrange("p (a n) -> p a n", n=128)), inc=False)
                xq_evac.append(vop(lambda e, t__=t_: e.tensor_copy(
                    xq_fm[:, 4:8, t__ * 128:(t__ + 1) * 128],
                    PSL2[:, :].rearrange("p (a n) -> p a n", n=128))))

            for t in range(NT):
                b = t % 2
                psu = [PSA, PSB][t % 2]
                tD = emit_oprojD(t)
                g_eng, g_val = xf2_guard[b]
                if g_eng == 'G':
                    ctx.waitd('y', 'G')
                else:
                    ctx.wait('y', g_eng, g_val)
                dma_y(xf2[b][:, :], x_in[xbase + t * 128: xbase + (t + 1) * 128, :],
                      sem='M%d' % b)
                ctx.wait('v', 'T', tD)
                ctx.waitd('v', 'M%d' % b)
                vxm = vop(lambda e, p_=psu, bb=b: e.tensor_tensor(
                    xf2[bb][:, :], p_[:, :], xf2[bb][:, :], OP.add))
                vxm_hist[t] = vxm
                ctx.wait('g', 'V', vxm)
                xmid_store[t] = dma_g(xmid_hbm[t * 128:(t + 1) * 128, :], xf2[b][:, :])
                xf2_guard[b] = ('G', xmid_store[t])
                ctx.wait('v', 'V', vxm)
                vop(lambda e, bb=b: e.bn_stats(bnb[:, 0:6], xf2[bb][:, 0:512]), inc=False)
                vbn1 = vop(lambda e, bb=b: e.bn_stats(bnb[:, 6:12], xf2[bb][:, 512:1024]))
                if t >= 1:
                    qD[t - 1].s_add()
                ctx.wait('v', 'V', vbn1)
                vbn2 = vop(lambda e: e.bn_aggr(mv[:, 0:2], bnb[:, 0:12]))
                if t >= 1:
                    qD[t - 1].s_mult(dst_war=trD_T.get(t - 3, 0))
                    emit_trD(t - 1)
                ctx.wait('s', 'V', vbn2)
                sop(lambda e: e.activation(lnv[:, 0:1], mv[:, 1:2], ACT.Ln,
                                           bias=epsb[:, 0:1]))
                ctx.wait('s', 'S', c['S'])
                srs2 = sop(lambda e: e.activation(rstd[:, 0:1], lnv[:, 0:1], ACT.Exp,
                                                  scale=-0.5))
                if t >= 1:
                    emit_evacD(t - 1)
                ctx.wait('v', 'S', srs2)
                if t >= 1:
                    ctx.wait('v', 'V', qD[t - 1].c_ops)
                vap2 = vop(lambda e, bb=b: e.tensor_scalar(
                    xn2f[:, :], xf2[bb][:, :], mv[:, 0:1], rstd[:, 0:1],
                    OP.subtract, OP.mult))
                g_eng, g_val = xf_guard[b]
                qw = qD[t - 2].c_mul if t >= 2 else 0
                q = Quant(xf[b], xn2f, 1024, war=qw)
                if g_eng == 'V':
                    pass
                q.s_reduce(src_done=vap2)
                q.s_scale()
                q.s_recip()
                q.s_ops()
                qD[t] = q
            qD[NT - 1].s_add()
            qD[NT - 1].s_mult(dst_war=trD_T.get(NT - 3, 0))
            emit_trD(NT - 1)
            emit_evacD(NT - 1)
            for b in range(2):
                xf_guard[b] = ('T', trD_T[NT - 1 if (NT - 1) % 2 == b else NT - 2])

            wd_l = dma_w(wd_sb[:, :, :], wdqb.rearrange("(a p) n -> p a n", p=128))
            ctx.wait('s', 'V', xq_evac[-1])
            wg_l = dma_w(wg_sb[:, :, :], wgqb.rearrange("(a p) n -> p a n", p=128))
            wu_l = dma_w(wu_sb[:, :, :], wuqb.rearrange("(a p) n -> p a n", p=128))

            ctx.wait('t', 'W', wu_l)
            ctx.wait('t', 'V', xq_evac[-1])
            PAIR_W = [1024, 1024, 768]
            IC_W = [512, 512, 512, 512, 512, 256]
            NPAIR = NT * 3
            qF = {}
            hm1 = {}
            trF = {}
            evF = {}
            dnF = {}
            vo2_hist = {}

            def emit_gateupF(pi):
                t_, pc_ = divmod(pi, 3)
                ctx.wait('t', 'V', hm1.get(pi - 1, 0))
                res = []
                for half in (0, 1):
                    ic = 2 * pc_ + half
                    ww = IC_W[ic]
                    psu = [PSA, PSB][half]
                    for wsb in (wg_sb, wu_sb):
                        for hc in range(8):
                            top(lambda e, p_=psu, c_=hc, w_=wsb, i_=ic, ww_=ww,
                                g_=(0 if wsb is wg_sb else 1), tt_=t_: e.matmul(
                                    p_[:, g_ * 512:g_ * 512 + ww_],
                                    xq_fm[:, c_, tt_ * 128:(tt_ + 1) * 128],
                                    w_[:, c_, i_ * 512:i_ * 512 + ww_],
                                    start=(c_ == 0), stop=(c_ == 7)),
                                inc=(hc == 7))
                    res.append(c['T'])
                return res

            def emit_hmults(pi, tgu):
                t_, pc_ = divmod(pi, 3)
                base = 0
                last = 0
                if pi == 0:
                    ctx.wait('s', 'V', xq_evac[-1])
                for half in (0, 1):
                    ic = 2 * pc_ + half
                    ww = IC_W[ic]
                    psu = [PSA, PSB][half]
                    ctx.wait('s', 'T', tgu[half])
                    if half == 1:
                        ctx.wait('s', 'V', c['V'])
                    ssg = sop(lambda e, p_=psu, ww_=ww: e.activation(
                        sgf[:, 0:ww_], p_[:, 0:ww_], ACT.Silu))
                    ctx.wait('v', 'S', ssg)
                    if half == 0 and pi >= 1:
                        ctx.wait('v', 'V', qF[pi - 1].c_ops)
                    last = vop(lambda e, p_=psu, ww_=ww, b_=base: e.tensor_tensor(
                        xn2f[:, b_:b_ + ww_], sgf[:, 0:ww_], p_[:, 512:512 + ww_],
                        OP.mult))
                    base += ww
                hm1[pi] = last
                return last

            def emit_evacF(pi):
                t_, pc_ = divmod(pi, 3)
                nsub = PAIR_W[pc_] // 128
                ctx.wait('v', 'T', trF[pi])
                ctx.wait('v', 'T', dnF.get(pi - 1, 0))
                n1 = min(nsub, 4)
                vop(lambda e, n_=n1: e.tensor_copy(hq_fm[:, 0:n_ * 128],
                                                   PSL1[:, 0:n_ * 128]),
                    inc=(nsub <= 4))
                if nsub > 4:
                    vop(lambda e, n_=nsub - 4: e.tensor_copy(
                        hq_fm[:, 512:512 + n_ * 128], PSL2[:, 0:n_ * 128]))
                evF[pi] = c['V']

            def emit_trF(pi):
                t_, pc_ = divmod(pi, 3)
                nsub = PAIR_W[pc_] // 128
                ctx.wait('t', 'V', qF[pi].c_mul)
                ctx.wait('t', 'V', evF.get(pi - 1, 0))
                for j in range(nsub):
                    psl = [PSL1, PSL2][j // 4]
                    ttranspose(psl[:, (j % 4) * 128:((j % 4) + 1) * 128],
                               xf2[pi % 2][:, j * 128:(j + 1) * 128],
                               inc=(j == nsub - 1))
                trF[pi] = c['T']

            def emit_downF(pi):
                t_, pc_ = divmod(pi, 3)
                nsub = PAIR_W[pc_] // 128
                ctx.wait('t', 'V', evF[pi])
                if pc_ == 0:
                    ctx.wait('t', 'V', vo2_hist.get(t_ - 1, 0))
                for j in range(nsub):
                    isub = 8 * pc_ + j
                    st_ = (isub == 0)
                    sp_ = (isub == 21)
                    top(lambda e, j_=j, i_=isub, st__=st_, sp__=sp_: e.matmul(
                        PSO1[:, :], hq_fm[:, j_ * 128:(j_ + 1) * 128],
                        wd_sb[:, i_, 0:512], start=st__, stop=sp__), inc=False)
                    top(lambda e, j_=j, i_=isub, st__=st_, sp__=sp_: e.matmul(
                        PSO2[:, :], hq_fm[:, j_ * 128:(j_ + 1) * 128],
                        wd_sb[:, i_, 512:1024], start=st__, stop=sp__),
                        inc=(j == nsub - 1))
                dnF[pi] = c['T']

            def emit_residF_load(t_):
                b = t_ % 2
                g_eng, g_val = xf_guard[b]
                if g_eng == 'G':
                    ctx.waitd('y', 'G')
                else:
                    ctx.wait('y', g_eng, g_val)
                ctx.waitd('y', 'G')
                dma_y(xf[b][:, :], xmid_hbm[t_ * 128:(t_ + 1) * 128, :],
                      sem='L%d' % b)

            def emit_residF(t_):
                b = t_ % 2
                tdn = dnF[3 * t_ + 2]
                ctx.wait('v', 'T', tdn)
                ctx.waitd('v', 'L%d' % b)
                vop(lambda e, b_=b: e.tensor_tensor(
                    xf[b_][:, 0:512], PSO1[:, :], xf[b_][:, 0:512], OP.add))
                vo2 = vop(lambda e, b_=b: e.tensor_tensor(
                    xf[b_][:, 512:1024], PSO2[:, :], xf[b_][:, 512:1024], OP.add))
                vo2_hist[t_] = vo2
                ctx.wait('g', 'V', vo2)
                gst = dma_g(y_out[xbase + t_ * 128: xbase + (t_ + 1) * 128, :],
                            xf[b][:, :])
                xf_guard[b] = ('G', gst)

            for pi in range(NPAIR):
                t_, pc_ = divmod(pi, 3)
                tgu = emit_gateupF(pi)
                if pi >= 1:
                    qF[pi - 1].s_add()
                    if pi <= 2:
                        ctx.waitd('v', 'G')
                    qF[pi - 1].s_mult(dst_war=trF.get(pi - 3, 0))
                    emit_trF(pi - 1)
                if pc_ == 0 and t_ >= 1:
                    emit_residF_load(t_ - 1)
                emit_hmults(pi, tgu)
                q = Quant(xf2[pi % 2], xn2f, PAIR_W[pc_])
                qF[pi] = q
                if pi >= 1:
                    emit_evacF(pi - 1)
                    if (pi - 1) % 3 == 0 and t_ >= 1:
                        emit_residF(t_ - 1)
                    emit_downF(pi - 1)
                q.s_reduce(src_done=hm1[pi])
                q.s_scale()
                q.s_recip()
                q.s_ops()
            pi = NPAIR - 1
            qF[pi].s_add()
            qF[pi].s_mult(dst_war=trF.get(pi - 2, 0))
            emit_trF(pi)
            emit_residF_load(NT - 1)
            emit_evacF(pi)
            emit_downF(pi)
            emit_residF(NT - 1)
            for b in range(2):
                last_pi = NPAIR - 1 if (NPAIR - 1) % 2 == b else NPAIR - 2
                xf2_guard[b] = ('T', trF[last_pi])
            prev_seq_V_end = c['V']



        ctx.on('g', lambda e: e.wait_ge(ctx.sem['G'], c['G']))

    with (
        nc.semaphore("semL0") as semL0,
        nc.semaphore("semL1") as semL1,
        nc.semaphore("semM0") as semM0,
        nc.semaphore("semM1") as semM1,
        nc.semaphore("semC") as semC,
        nc.semaphore("semW") as semW,
        nc.semaphore("semG") as semG,
        nc.semaphore("semT") as semT,
        nc.semaphore("semV") as semV,
        nc.semaphore("semS") as semS,
        nc.Block() as block,
    ):
        sems = {"L0": semL0, "L1": semL1, "M0": semM0, "M1": semM1, "C": semC,
                "W": semW, "G": semG, "T": semT, "V": semV, "S": semS}

        @block.sync
        def _(eng):
            emit(Ctx('y', eng, sems))

        @block.vector
        def _(eng):
            emit(Ctx('v', eng, sems))

        @block.scalar
        def _(eng):
            emit(Ctx('s', eng, sems))

        @block.tensor
        def _(eng):
            emit(Ctx('t', eng, sems))

        @block.gpsimd
        def _(eng):
            emit(Ctx('g', eng, sems))

    mybir.codegen_inst_isa_subclasses(nc)
    return nc


_NC_CACHE = {}


def kernel(L0_attn_hidden_states, L0_attn_Wq, L0_attn_Wkv, L0_attn_Wo, L0_attn_bo,
           L0_ffn_w_gate, L0_ffn_w_up, L0_ffn_w_down):
    x = np.ascontiguousarray(np.asarray(L0_attn_hidden_states, dtype=np.float32))
    B = x.shape[0]
    wq = np.ascontiguousarray(np.asarray(L0_attn_Wq, dtype=np.float32))
    wkv = np.asarray(L0_attn_Wkv, dtype=np.float32)
    wk = np.ascontiguousarray(wkv[:, :D])
    wv = np.ascontiguousarray(wkv[:, D:])
    wo = np.ascontiguousarray(np.asarray(L0_attn_Wo, dtype=np.float32))
    bo = np.ascontiguousarray(np.asarray(L0_attn_bo, dtype=np.float32)).reshape(1, D)
    wg = np.ascontiguousarray(np.asarray(L0_ffn_w_gate, dtype=np.float32))
    wu = np.ascontiguousarray(np.asarray(L0_ffn_w_up, dtype=np.float32))
    wd = np.ascontiguousarray(np.asarray(L0_ffn_w_down, dtype=np.float32))
    ident = np.eye(128, dtype=np.float32)
    onesb = np.ones((128, 128), dtype=ml_dtypes.bfloat16)

    if "nc" not in _NC_CACHE:
        _NC_CACHE["nc"] = build_program()
    nc = _NC_CACHE["nc"]

    n_cores = 8
    per = B // n_cores
    in_maps = []
    for ci in range(n_cores):
        shard = np.ascontiguousarray(x[ci * per:(ci + 1) * per].reshape(per * S, D))
        in_maps.append({
            "x": shard, "wq": wq, "wk": wk, "wv": wv, "wo": wo, "bo": bo,
            "wg": wg, "wu": wu, "wd": wd, "ident": ident, "onesb": onesb,
        })
    res = run_bass_kernel_spmd(nc, in_maps, core_ids=list(range(n_cores)))
    return np.concatenate([r["y"].reshape(per, S, D) for r in res.results], axis=0)


# version 09
# speedup vs baseline: 1.2239x; 1.2239x over previous
import numpy as np
import ml_dtypes

import concourse.bass as bass
import concourse.mybir as mybir
from concourse.bass_utils import run_bass_kernel_spmd
from concourse.dve_spec import Spec, Src0, Src1, C0, C1, Zero, One, maxx, minn, lower
from concourse.dve_op import DveOp
import concourse.dve_ops as dve_ops

F32 = mybir.dt.float32
BF16 = mybir.dt.bfloat16
AX = mybir.AxisListType
OP = mybir.AluOpType
ACT = mybir.ActivationFunctionType

C_VELT = float(2 ** 22 + 1)
C_MAGIC = float(0.75 * 2 ** 23)
_GRID = np.array([0.0, 0.5, 1.0, 1.5, 2.0, 3.0, 4.0, 6.0])
_MID = np.array([0.25, 0.75, 1.25, 1.75, 2.5, 3.5, 5.0])


def _fp4_ref(in0, in1, s0, s1, imm2, part):
    f = np.float64 if in0.dtype == np.float64 else np.float32
    xn = in0.astype(f) * in1.astype(f)
    if f is np.float64:
        idx = np.searchsorted(_MID, np.abs(xn).ravel()).reshape(np.shape(xn))
        r = _GRID[idx] * np.sign(xn)
    else:
        xn = xn.astype(np.float32)
        c = (xn * np.float32(C_VELT)).astype(np.float32)
        d = (c - xn).astype(np.float32)
        hi = (c - d).astype(np.float32)
        l = ((xn + np.float32(C_MAGIC)) - np.float32(C_MAGIC)).astype(np.float32)
        r = (hi - np.clip(hi, -2, 2)) + np.clip(l, -2, 2)
    out = (r - np.clip(r, -2.0, 2.0)) if part == "ovf" else np.clip(r, -2.0, 2.0)
    return out.astype(f)


def _mk_bodyA():
    xn = Src0 * Src1
    c = xn * C1
    d = c - xn
    hi = c - d
    m = minn(maxx(hi, Zero - (One + One)), One + One)
    return hi - m


def _mk_bodyB():
    xn = Src0 * Src1
    l = (xn + C0) - C0
    return minn(maxx(l, Zero - (One + One)), One + One)


def _mk_bodyS():
    return maxx(Src0, C0) * C1


def _scale6_ref(in0, in1, s0, s1, imm2):
    if in0.dtype == np.float64:
        return np.maximum(in0, np.float64(np.float32(6e-12))) / 6.0
    return (np.maximum(in0.astype(np.float32), np.float32(s0)) *
            np.float32(s1)).astype(np.float32)


def _register_fp4_ops():
    have = {op.name: op for op in dve_ops.OPS}
    if "FP4OVF" in have:
        return have["FP4OVF"], have["FP4LOW"], have["FP4SCALE"]
    shaA = {v: lower(Spec(body=_mk_bodyA()), ver=v, perf_en=False, name="FP4OVF").sha(v)
            for v in ("v3", "v4")}
    shaB = {v: lower(Spec(body=_mk_bodyB()), ver=v, perf_en=False, name="FP4LOW").sha(v)
            for v in ("v3", "v4")}
    opA = DveOp("FP4OVF", Spec(body=_mk_bodyA()), subdim=False, uops_sha=shaA,
                reference=lambda i0, i1, s0, s1, imm2: _fp4_ref(i0, i1, s0, s1, imm2, "ovf"))
    opB = DveOp("FP4LOW", Spec(body=_mk_bodyB()), subdim=False, uops_sha=shaB,
                reference=lambda i0, i1, s0, s1, imm2: _fp4_ref(i0, i1, s0, s1, imm2, "low"))
    shaS = {v: lower(Spec(body=_mk_bodyS()), ver=v, perf_en=False, name="FP4SCALE").sha(v)
            for v in ("v3", "v4")}
    opS = DveOp("FP4SCALE", Spec(body=_mk_bodyS()), subdim=False, uops_sha=shaS,
                reference=_scale6_ref)
    dve_ops.OPS.append(opA)
    dve_ops.OPS.append(opB)
    dve_ops.OPS.append(opS)
    dve_ops._SUB_OPCODE_FOR_NAME["FP4OVF"] = 31
    dve_ops._SUB_OPCODE_FOR_NAME["FP4LOW"] = 30
    dve_ops._SUB_OPCODE_FOR_NAME["FP4SCALE"] = 29
    for o in (opA, opB, opS):
        dve_ops.CUSTOM_DVE_SPECS[o.name] = o.spec
        dve_ops.CUSTOM_DVE_REFS[o.name] = o.ref
    return opA, opB, opS


OPA, OPB, OPS6 = _register_fp4_ops()

NSEQ = 4
STOP = 'Z'
S = 2048
D = 1024
NH = 16
I = 2816
NT = S // 128
NHP = NH // 2
NG = S // 64
AM_FLOOR = float(np.float32(6e-12))
FLOOR12 = float(np.float32(1e-12))
C6 = float(np.float32(1.0) / np.float32(6.0))


class Ctx:

    def __init__(self, active, engine, sems):
        self.a = active
        self.eng = engine
        self.sem = sems
        self.c = {k: 0 for k in sems}

    def on(self, eng, fn, inc=None, n=1):
        if self.a == eng:
            inst = fn(self.eng)
            if inc is not None:
                inst.then_inc(self.sem[inc], n)
        if inc is not None:
            self.c[inc] += n
            return self.c[inc]
        return None

    def wait(self, eng, sem, val):
        if val and val > 0 and self.a == eng:
            self.eng.wait_ge(self.sem[sem], val)

    def waitd(self, eng, sem):
        self.wait(eng, sem, self.c[sem])


def build_program():
    nc = bass.Bass("TRN2", target_bir_lowering=False, debug=False)

    x_in = nc.dram_tensor("x", [NSEQ * S, D], F32, kind="ExternalInput").ap()
    wq_in = nc.dram_tensor("wq", [D, D], F32, kind="ExternalInput").ap()
    wk_in = nc.dram_tensor("wk", [D, D], F32, kind="ExternalInput").ap()
    wv_in = nc.dram_tensor("wv", [D, D], F32, kind="ExternalInput").ap()
    wo_in = nc.dram_tensor("wo", [D, D], F32, kind="ExternalInput").ap()
    bo_in = nc.dram_tensor("bo", [1, D], F32, kind="ExternalInput").ap()
    wg_in = nc.dram_tensor("wg", [I, D], F32, kind="ExternalInput").ap()
    wu_in = nc.dram_tensor("wu", [I, D], F32, kind="ExternalInput").ap()
    wd_in = nc.dram_tensor("wd", [D, I], F32, kind="ExternalInput").ap()
    ident_in = nc.dram_tensor("ident", [128, 128], F32, kind="ExternalInput").ap()
    ones_in = nc.dram_tensor("onesb", [128, 128], BF16, kind="ExternalInput").ap()
    y_out = nc.dram_tensor("y", [NSEQ * S, D], F32, kind="ExternalOutput").ap()

    wqb = nc.dram_tensor("wqb", [D, D], BF16, kind="Internal").ap()
    wkb = nc.dram_tensor("wkb", [D, D], BF16, kind="Internal").ap()
    wvb = nc.dram_tensor("wvb", [D, D], BF16, kind="Internal").ap()
    wob = nc.dram_tensor("wob", [D, D], BF16, kind="Internal").ap()
    wgqb = nc.dram_tensor("wgqb", [D, I], BF16, kind="Internal").ap()
    wuqb = nc.dram_tensor("wuqb", [D, I], BF16, kind="Internal").ap()
    wdqb = nc.dram_tensor("wdqb", [I, D], BF16, kind="Internal").ap()
    xmid_hbm = nc.dram_tensor("xmidh", [S, D], F32, kind="Internal").ap()

    abf = nc.alloc_sbuf_tensor("abf", [128, 88 * 1024], BF16)
    af = nc.alloc_sbuf_tensor("af", [128, 7424], F32)
    ident = nc.alloc_sbuf_tensor("identsb", [128, 128], F32)
    onesb = nc.alloc_sbuf_tensor("onessb", [128, 128], BF16)
    bob = nc.alloc_sbuf_tensor("bobsb", [1, 1024], BF16)
    AB, AF = abf.ap(), af.ap()

    def bfv(off, shape):
        n = int(np.prod(shape))
        v = AB[:, off:off + n]
        if len(shape) > 1:
            v = v.rearrange("p (a b) -> p a b", b=shape[-1])
        return v

    def fv(off, shape):
        n = int(np.prod(shape))
        v = AF[:, off:off + n]
        if len(shape) > 1:
            v = v.rearrange("p (a b) -> p a b", b=shape[-1])
        return v

    K_ = 1024
    xn_fm = bfv(0, [8, S])
    xq_fm = xn_fm
    v_sb = bfv(16 * K_, [NT, 1024])
    wd_sb = bfv(16 * K_, [22, 1024])
    q_sb = [bfv(32 * K_ + i * 4 * K_, [S]) for i in range(2)]
    k_sb = [bfv(40 * K_ + i * 4 * K_, [S]) for i in range(2)]
    o_sb = bfv(48 * K_, [8, S])
    wg_sb = bfv(40 * K_, [8, I])
    et = [bfv(64 * K_ + i * K_, [1024]) for i in range(2)]
    wq_sb = bfv(66 * K_, [8, 1024])
    wo_sb = wq_sb
    wk_sb = bfv(74 * K_, [8, 1024])
    wu_sb = bfv(62 * K_ + 512, [8, I])
    stg_b = [bfv(85 * K_ + i * K_, [1024]) for i in range(2)]
    qt_ovf = bfv(85 * K_, [1024])
    qt_low = bfv(86 * K_, [1024])
    qt_shb = bfv(87 * K_, [176])
    hq_fm = bfv(38 * K_, [1024])
    wfm = bfv(24 * K_, [8, I])
    wfm_d = bfv(24 * K_, [22, 1024])
    sq_ovf = bfv(48 * K_, [I])
    sq_low = bfv(51 * K_, [I])
    sq_shb = bfv(54 * K_, [176])

    xf = [fv(i * K_, [1024]) for i in range(2)]
    xf2 = [fv(2 * K_ + i * K_, [1024]) for i in range(2)]
    xn2f = fv(4 * K_, [1024])
    lrec = fv(5 * K_, [512])
    osum = fv(5 * K_ + 512, [512])
    sgf = fv(6 * K_, [512])
    wdstg = [fv(0, [I]), fv(3 * K_, [I])]
    bnb = fv(6656, [12])
    mv = fv(6672, [2])
    lnv = fv(6676, [1])
    rstd = fv(6680, [1])
    am_t = fv(6688, [176])
    epsb = fv(7232, [1])
    sh_t = fv(6864, [176])
    rc_t = fv(7040, [176])

    ps_a = nc.alloc_psum_tensor("ps_a", [128, 1024], F32)
    ps_b = nc.alloc_psum_tensor("ps_b", [128, 1024], F32)
    ps_o1 = nc.alloc_psum_tensor("ps_o1", [128, 512], F32)
    ps_o2 = nc.alloc_psum_tensor("ps_o2", [128, 512], F32)
    ps_l1 = nc.alloc_psum_tensor("ps_l1", [128, 512], F32)
    ps_l2 = nc.alloc_psum_tensor("ps_l2", [128, 512], F32)
    PSA, PSB = ps_a.ap(), ps_b.ap()
    PSO1, PSO2, PSL1, PSL2 = ps_o1.ap(), ps_o2.ap(), ps_l1.ap(), ps_l2.ap()
    IDT = ident.ap()
    ONE = onesb.ap()
    BOB = bob.ap()

    def emit(ctx: Ctx):
        c = ctx.c

        def dma_y(dst, src, sem='C'):
            return ctx.on('y', lambda e: e.dma_start(out=dst, in_=src), inc=sem, n=16)

        def dma_w(dst, src):
            return ctx.on('s', lambda e: e.dma_start(out=dst, in_=src), inc='W', n=16)

        def dma_g(dst, src):
            return ctx.on('g', lambda e: e.dma_start(out=dst, in_=src), inc='G', n=16)

        def vop(fn, inc=True):
            return ctx.on('v', fn, inc='V' if inc else None, n=1)

        def sop(fn, inc=True):
            return ctx.on('s', fn, inc='S' if inc else None, n=1)

        def top(fn, inc=True):
            return ctx.on('t', fn, inc='T' if inc else None, n=1)

        def vdrain():
            ctx.wait('v', 'V', c['V'])

        def ttranspose(out_ap, in_ap, inc=False):
            return top(lambda e: e.matmul(out_ap, in_ap, IDT[:, :],
                                          is_transpose=True, skip_group_check=True),
                       inc=inc)

        class Quant:

            def __init__(self, dst, src_f32, width, scr=None, war=0):
                self.dst = dst
                self.src = src_f32[:, 0:width].rearrange("p (b k) -> p b k", k=16)
                self.w = width
                self.nb = width // 16
                self.scr = scr if scr is not None else (qt_ovf, qt_low, qt_shb)
                self.war = war

            def s_reduce(self, src_done=0):
                ctx.wait('v', 'V', max(self.war, src_done))
                self.c_red = vop(lambda e: e.tensor_reduce(
                    am_t[:, 0:self.nb], self.src, axis=AX.X, op=OP.max,
                    apply_absolute_value=True))

            def s_scale(self):
                ctx.wait('v', 'V', self.c_red)
                self.c_sc = vop(lambda e: e._custom_dve(
                    OPS6, out=sh_t[:, 0:self.nb], in0=am_t[:, 0:self.nb],
                    s0=AM_FLOOR, s1=C6))

            def s_recip(self):
                ctx.wait('v', 'V', self.c_sc)
                vop(lambda e: e.tensor_copy(self.scr[2][:, 0:self.nb],
                                            sh_t[:, 0:self.nb]), inc=False)
                self.c_rc = vop(lambda e: e.reciprocal_approx_fast(
                    rc_t[:, 0:self.nb], sh_t[:, 0:self.nb]))

            def s_ops(self):
                ctx.wait('v', 'V', self.c_rc)
                rcb = rc_t[:, 0:self.nb].broadcast_to([128, self.nb, 16])
                vop(lambda e, r_=rcb: e._custom_dve(
                    OPA, out=self.scr[0][:, 0:self.w].rearrange("p (b k) -> p b k", k=16),
                    in0=self.src, in1=r_, s0=0.0, s1=C_VELT), inc=False)
                self.c_ops = vop(lambda e, r_=rcb: e._custom_dve(
                    OPB, out=self.scr[1][:, 0:self.w].rearrange("p (b k) -> p b k", k=16),
                    in0=self.src, in1=r_, s0=C_MAGIC, s1=0.0))

            def s_add(self):
                ctx.wait('v', 'V', self.c_ops)
                self.c_add = vop(lambda e: e.tensor_tensor(
                    self.scr[0][:, 0:self.w], self.scr[0][:, 0:self.w],
                    self.scr[1][:, 0:self.w], OP.add))

            def s_mult(self, dst_war=0):
                ctx.wait('v', 'V', self.c_add)
                if dst_war:
                    ctx.wait('v', 'T', dst_war)
                shb = self.scr[2][:, 0:self.nb].broadcast_to([128, self.nb, 16])
                self.c_mul = vop(lambda e: e.tensor_tensor(
                    self.dst[:, 0:self.w].rearrange("p (b k) -> p b k", k=16),
                    self.scr[0][:, 0:self.w].rearrange("p (b k) -> p b k", k=16),
                    shb, OP.mult))
                return self.c_mul

        def quant_tile(dst_bf, src_f32, width, scr=None, war=0, src_done=0,
                       dst_war=0):
            q = Quant(dst_bf, src_f32, width, scr=scr, war=war)
            q.s_reduce(src_done)
            q.s_scale()
            q.s_recip()
            q.s_ops()
            q.s_add()
            return q.s_mult(dst_war)

        def ttranspose(out_ap, in_ap, inc=False):
            return top(lambda e: e.matmul(out_ap, in_ap, IDT[:, :],
                                          is_transpose=True, skip_group_check=True),
                       inc=inc)

        def quant_tile(dst_bf, src_f32, width, scr=None):
            t_ovf, t_low, t_shb = scr if scr is not None else (qt_ovf, qt_low, qt_shb)
            nb = width // 16
            sv = src_f32[:, 0:width].rearrange("p (b k) -> p b k", k=16)
            vdrain()
            vop(lambda e: e.tensor_reduce(am_t[:, 0:nb], sv, axis=AX.X, op=OP.max,
                                          apply_absolute_value=True))
            vdrain()
            vop(lambda e: e._custom_dve(OPS6, out=sh_t[:, 0:nb], in0=am_t[:, 0:nb],
                                        s0=AM_FLOOR, s1=C6))
            vdrain()
            vop(lambda e: e.tensor_copy(t_shb[:, 0:nb], sh_t[:, 0:nb]))
            vop(lambda e: e.reciprocal_approx_fast(rc_t[:, 0:nb], sh_t[:, 0:nb]))
            vdrain()
            rcb = rc_t[:, 0:nb].broadcast_to([128, nb, 16])
            shb = t_shb[:, 0:nb].broadcast_to([128, nb, 16])
            vop(lambda e: e._custom_dve(
                OPA, out=t_ovf[:, 0:width].rearrange("p (b k) -> p b k", k=16),
                in0=sv, in1=rcb, s0=0.0, s1=C_VELT))
            vop(lambda e: e._custom_dve(
                OPB, out=t_low[:, 0:width].rearrange("p (b k) -> p b k", k=16),
                in0=sv, in1=rcb, s0=C_MAGIC, s1=0.0))
            vdrain()
            vop(lambda e: e.tensor_tensor(t_ovf[:, 0:width], t_ovf[:, 0:width],
                                          t_low[:, 0:width], OP.add))
            vdrain()
            return vop(lambda e: e.tensor_tensor(
                dst_bf[:, 0:width].rearrange("p (b k) -> p b k", k=16),
                t_ovf[:, 0:width].rearrange("p (b k) -> p b k", k=16),
                shb, OP.mult))

        ctx.on('v', lambda e: e.memset(epsb[:, 0:1], 1e-5))
        dma_y(IDT[:, :], ident_in[:, :])
        dma_y(ONE[:, :], ones_in[:, :])
        cl_bo = dma_y(xf[0][0:1, :], bo_in[0:1, :])
        ctx.waitd('v', 'C')
        v_bo = vop(lambda e: e.tensor_copy(BOB[0:1, :], xf[0][0:1, :]))

        ctx.waitd('t', 'C')

        cast_v = {}

        def attn_cast_load(i):
            b = i % 2
            wi, ch = divmod(i, 8)
            src = (wq_in, wk_in, wv_in, wo_in)[wi]
            ctx.wait('y', 'V', cast_v[i - 2] if i >= 2 else v_bo)
            dma_y(xf[b][:, :], src[ch * 128:(ch + 1) * 128, :], sem='L%d' % b)

        def attn_cast_do(i):
            b = i % 2
            wi, ch = divmod(i, 8)
            dst = (wqb, wkb, wvb, wob)[wi]
            ctx.waitd('v', 'L%d' % b)
            if i >= 2:
                ctx.waitd('v', 'C')
            cast_v[i] = vop(lambda e, b_=b: e.tensor_copy(stg_b[b_][:, :],
                                                          xf[b_][:, :]))
            ctx.wait('y', 'V', cast_v[i])
            ctx.on('y', lambda e, d_=dst, c_=ch, b_=b: e.dma_start(
                out=d_[c_ * 128:(c_ + 1) * 128, :], in_=stg_b[b_][:, :]),
                inc='C', n=16)

        CH = []
        for wsel, (src_w, dstb) in enumerate(((wg_in, wgqb), (wu_in, wuqb),
                                              (wd_in, wdqb))):
            for chq in range(8 if wsel == 2 else 22):
                CH.append((wsel, chq, src_w, dstb))
        NCH = len(CH)
        scrA = (sq_ovf, sq_low, sq_shb)
        scrB = (bfv(0, [I]), bfv(3 * K_, [I]), bfv(6 * K_, [176]))

        def stg_of(ci):
            wsel = CH[ci][0]
            return ((wdstg[ci % 2], I) if wsel == 2 else (xf2[ci % 2], 1024))

        tr_chunk = {}
        ev_group = {}
        g_state = [0]
        qS = {}

        def load_ch(ci):
            wsel, chq, src_w, dstb = CH[ci]
            dst, wid = stg_of(ci)
            if ci - 2 in tr_chunk:
                ctx.wait('s', 'T', tr_chunk[ci - 2])
            if ci in (44, 45):
                ctx.wait('s', 'V', cast_v[31])
            ctx.on('s', lambda e, d_=dst, w_=wid, s_=src_w, q_=chq: e.dma_start(
                out=d_[:, 0:w_], in_=s_[q_ * 128:(q_ + 1) * 128, :]),
                inc='M%d' % (ci % 2), n=16)

        def tr_evac_ch(ci):
            wsel, chq, src_w, dstb = CH[ci]
            dst, wid = stg_of(ci)
            ncol = wid // 128
            fm = wfm_d if wsel == 2 else wfm
            nch_w = 8 if wsel == 2 else 22
            ctx.wait('t', 'V', qS[ci].c_mul)
            for b0 in range(0, ncol, 8):
                nb8 = min(8, ncol - b0)
                g = g_state[0]
                psu = [PSA, PSB][g % 2]
                if g - 2 in ev_group:
                    ctx.wait('t', 'S', ev_group[g - 2])
                for j in range(nb8):
                    bb = b0 + j
                    ttranspose(psu[:, j * 128:(j + 1) * 128],
                               dst[:, bb * 128:(bb + 1) * 128],
                               inc=(j == nb8 - 1))
                ctx.wait('s', 'T', c['T'])
                if b0 == 0 and chq == 0:
                    ctx.waitd('s', 'G')
                ctx.on('s', lambda e, p_=psu, n_=nb8, f_=fm, b_=b0, q_=chq:
                       e.activation(f_[:, b_:b_ + n_, q_ * 128:(q_ + 1) * 128],
                                    p_[:, 0:n_ * 128].rearrange(
                                        "p (a n) -> p a n", n=128),
                                    ACT.Copy),
                       inc='S', n=1)
                ev_group[g] = c['S']
                g_state[0] = g + 1
            tr_chunk[ci] = c['T']
            if chq == nch_w - 1:
                ctx.wait('g', 'S', c['S'])
                dma_g(dstb.rearrange("(a p) n -> p a n", p=128), fm[:, :, :])

        attn_cast_load(0)
        attn_cast_load(1)
        load_ch(0)
        load_ch(1)
        for ci in range(NCH):
            if ci >= 1:
                qS[ci - 1].s_add()
                qS[ci - 1].s_mult()
                tr_evac_ch(ci - 1)
                if ci + 1 < NCH:
                    load_ch(ci + 1)
            if ci < 32:
                attn_cast_do(ci)
            if ci + 2 <= 31:
                attn_cast_load(ci + 2)
            dst, wid = stg_of(ci)
            ctx.waitd('v', 'M%d' % (ci % 2))
            q = Quant(dst, dst, wid, scr=(scrA if ci % 2 == 0 else scrB))
            qS[ci] = q
            q.s_reduce(src_done=(qS[ci - 1].c_mul if ci >= 1 else 0))
            q.s_scale()
            q.s_recip()
            q.s_ops()
        qS[NCH - 1].s_add()
        qS[NCH - 1].s_mult()
        tr_evac_ch(NCH - 1)
        last_attnw_cast = cast_v[31]
        setup_done_G = c['G']
        setup_done_V = c['V']
        setup_done_T = c['T']
        setup_done_S = c['S']

        if STOP == 'S':
            ctx.on('g', lambda e: e.wait_ge(ctx.sem['G'], c['G']))
            return
        ctx.wait('v', 'T', setup_done_T)
        ctx.wait('t', 'S', setup_done_S)
        xf_guard = [('T', tr_chunk[50]), ('T', tr_chunk[50])]
        xf2_guard = [('T', tr_chunk[50]), ('T', tr_chunk[51])]
        prev_seq_V_end = 0

        for s_i in range(NSEQ):
            xbase = s_i * S
            ctx.waitd('s', 'G')
            ctx.waitd('s', 'C')
            wq_l = dma_w(wq_sb[:, :, :], wqb.rearrange("(a p) n -> p a n", p=128))
            wk_l = dma_w(wk_sb[:, :, :], wkb.rearrange("(a p) n -> p a n", p=128))
            wv_l = dma_w(v_sb[:, 0, 0:1024] if False else bfv(48 * K_, [8, 1024])[:, :, :],
                         wvb.rearrange("(a p) n -> p a n", p=128))
            wv_sb = bfv(48 * K_, [8, 1024])

            evacA_S = {}
            trA_T = {}
            vapA = {}

            def loadA(t):
                b = t % 2
                g_eng, g_val = xf_guard[b]
                if g_eng == 'G':
                    ctx.waitd('y', 'G')
                else:
                    ctx.wait('y', g_eng, g_val)
                dma_y(xf[b][:, :], x_in[xbase + t * 128: xbase + (t + 1) * 128, :],
                      sem='L%d' % b)

            def trevA(t):
                ctx.wait('t', 'V', vapA[t])
                if t >= 2:
                    ctx.wait('t', 'S', evacA_S[t - 2])
                elif s_i > 0:
                    ctx.wait('t', 'V', prev_seq_V_end)
                psT = [PSA, PSB][t % 2]
                for j in range(8):
                    ttranspose(psT[:, j * 128:(j + 1) * 128],
                               xn2f[:, j * 128:(j + 1) * 128], inc=(j == 7))
                trA_T[t] = c['T']
                ctx.wait('s', 'T', c['T'])
                ctx.on('s', lambda e, t_=t, p_=psT: e.activation(
                    xn_fm[:, :, t_ * 128:(t_ + 1) * 128],
                    p_[:, :].rearrange("p (a n) -> p a n", n=128), ACT.Copy),
                    inc='S', n=1)
                evacA_S[t] = c['S']

            loadA(0)
            for t in range(NT):
                b = t % 2
                if t + 1 < NT:
                    loadA(t + 1)
                ctx.waitd('v', 'L%d' % b)
                vop(lambda e, b_=b: e.bn_stats(bnb[:, 0:6], xf[b_][:, 0:512]),
                    inc=False)
                vop(lambda e, b_=b: e.bn_stats(bnb[:, 6:12], xf[b_][:, 512:1024]))
                vdrain()
                vbn = vop(lambda e: e.bn_aggr(mv[:, 0:2], bnb[:, 0:12]))
                ctx.wait('s', 'V', vbn)
                sop(lambda e: e.activation(lnv[:, 0:1], mv[:, 1:2], ACT.Ln, bias=epsb[:, 0:1]))
                ctx.wait('s', 'S', c['S'])
                srs = sop(lambda e: e.activation(rstd[:, 0:1], lnv[:, 0:1], ACT.Exp,
                                                 scale=-0.5))
                if t >= 1:
                    trevA(t - 1)
                ctx.wait('v', 'S', srs)
                if t >= 1:
                    ctx.wait('v', 'T', trA_T[t - 1])
                vap = vop(lambda e, b_=b: e.tensor_scalar(
                    xn2f[:, :], xf[b_][:, :], mv[:, 0:1], rstd[:, 0:1],
                    OP.subtract, OP.mult))
                vapA[t] = vap
                xf_guard[b] = ('V', vap)
            trevA(NT - 1)

            if STOP == 'A':
                continue
            ctx.waitd('t', 'W')
            vprod_ev = []
            for t in range(NT):
                psu = [PSA, PSB][t % 2]
                ctx.wait('t', 'S', evacA_S[NT - 1])
                if t >= 2:
                    ctx.wait('t', 'V', vprod_ev[t - 2])
                for n in range(2):
                    for dc in range(8):
                        top(lambda e, p_=psu, n_=n, d_=dc, t_=t: e.matmul(
                            p_[:, n_ * 512:(n_ + 1) * 512],
                            xn_fm[:, d_, t_ * 128:(t_ + 1) * 128],
                            wv_sb[:, d_, n_ * 512:(n_ + 1) * 512],
                            start=(d_ == 0), stop=(d_ == 7)),
                            inc=(n == 1 and d_ == 7) if False else (n == 1 and dc == 7))
                ctx.wait('v', 'T', c['T'])
                vprod_ev.append(vop(lambda e, p_=psu, t_=t: e.tensor_copy(
                    v_sb[:, t_, :], p_[:, :])))

            if STOP == 'B':
                continue
            ctx.waitd('t', 'W')
            last_exp_bank = [0, 0]
            last_pv_et = [0, 0]
            ep_free_v = 0
            ep_psum_free = 0
            last_exp_all = 0
            for hp in range(NHP):
                pb = hp % 2
                for wi, (wsb, dsb, psu, par) in enumerate(
                        ((wq_sb, q_sb[pb], PSA, 0), (wk_sb, k_sb[pb], PSB, 1))):
                    ctx.wait('t', 'S', last_exp_bank[par])
                    if hp == 0:
                        ctx.wait('t', 'V', vprod_ev[-1])
                    ve_half = [0, 0]
                    for n in range(4):
                        if n >= 2:
                            ctx.wait('t', 'V', ve_half[n % 2])
                        for dc in range(8):
                            top(lambda e, p_=psu, w_=wsb, n_=n, d_=dc: e.matmul(
                                p_[:, (n_ % 2) * 512:((n_ % 2) + 1) * 512],
                                w_[:, d_, hp * 128:(hp + 1) * 128],
                                xn_fm[:, d_, n_ * 512:(n_ + 1) * 512],
                                start=(d_ == 0), stop=(d_ == 7)),
                                inc=(dc == 7))
                        ctx.wait('v', 'T', c['T'])
                        ve_half[n % 2] = vop(lambda e, p_=psu, d_=dsb, n_=n: e.tensor_copy(
                            d_[:, n_ * 512:(n_ + 1) * 512],
                            p_[:, (n_ % 2) * 512:((n_ % 2) + 1) * 512]))
                qk_ready = c['V']
                if STOP == 'C1':
                    continue

                for qc in range(4):
                    q0 = qc * 512
                    pv_args = {}

                    def emit_S_exp(up):
                        ge, go = 2 * up, 2 * up + 1
                        par = up % 2
                        sb = [PSA, PSB][par]
                        eb = et[par]
                        if up == 0:
                            ctx.wait('t', 'V', qk_ready)
                        ctx.wait('t', 'S', last_exp_bank[par])
                        top(lambda e, s_=sb: e.matmul(
                            s_[0:64, 0:512], k_sb[pb][0:64, ge * 64:(ge + 1) * 64],
                            q_sb[pb][0:64, q0:q0 + 512], start=True, stop=True,
                            skip_group_check=True), inc=False)
                        top(lambda e, s_=sb: e.matmul(
                            s_[64:128, 0:512], k_sb[pb][0:64, go * 64:(go + 1) * 64],
                            q_sb[pb][0:64, q0:q0 + 512], start=True, stop=True,
                            skip_group_check=True), inc=False)
                        top(lambda e, s_=sb: e.matmul(
                            s_[0:64, 512:1024], k_sb[pb][64:128, ge * 64:(ge + 1) * 64],
                            q_sb[pb][64:128, q0:q0 + 512], start=True, stop=True,
                            skip_group_check=True), inc=False)
                        tS = top(lambda e, s_=sb: e.matmul(
                            s_[64:128, 512:1024], k_sb[pb][64:128, go * 64:(go + 1) * 64],
                            q_sb[pb][64:128, q0:q0 + 512], start=True, stop=True,
                            skip_group_check=True))
                        ctx.wait('s', 'T', tS)
                        ctx.wait('s', 'T', last_pv_et[par])
                        se = sop(lambda e, s_=sb, b_=par: e.activation(
                            et[b_][:, :], s_[:, :], ACT.Exp, scale=0.125))
                        last_exp_bank[par] = se
                        pv_args[up] = (ge, go, par, se)
                        return se

                    def emit_PVl(up):
                        ge, go, par, se = pv_args[up]
                        eb = et[par]
                        ctx.wait('t', 'S', se)
                        if up == 0:
                            ctx.wait('t', 'V', ep_psum_free)
                        ste = (up == 0)
                        stp = (up == NG // 2 - 1)
                        vA_e = v_sb[0:64, ge // 2, hp * 128:hp * 128 + 64]
                        vB_e = v_sb[0:64, ge // 2, hp * 128 + 64:hp * 128 + 128]
                        vA_o = v_sb[64:128, go // 2, hp * 128:hp * 128 + 64]
                        vB_o = v_sb[64:128, go // 2, hp * 128 + 64:hp * 128 + 128]
                        top(lambda e, v_=vA_e, b_=eb: e.matmul(
                            PSO1[0:64, :], v_, b_[0:64, 0:512], start=ste, stop=stp,
                            skip_group_check=True), inc=False)
                        top(lambda e, v_=vB_e, b_=eb: e.matmul(
                            PSO1[64:128, :], v_, b_[0:64, 512:1024], start=ste, stop=stp,
                            skip_group_check=True), inc=False)
                        top(lambda e, b_=eb: e.matmul(
                            PSL1[0:64, :], ONE[64:128, 0:64], b_[64:128, 0:512],
                            start=ste, stop=stp, skip_group_check=True), inc=False)
                        top(lambda e, b_=eb: e.matmul(
                            PSL1[64:128, :], ONE[64:128, 0:64], b_[64:128, 512:1024],
                            start=ste, stop=stp, skip_group_check=True), inc=False)
                        top(lambda e, v_=vA_o, b_=eb: e.matmul(
                            PSO2[0:64, :], v_, b_[64:128, 0:512], start=ste, stop=stp,
                            skip_group_check=True), inc=False)
                        top(lambda e, v_=vB_o, b_=eb: e.matmul(
                            PSO2[64:128, :], v_, b_[64:128, 512:1024], start=ste, stop=stp,
                            skip_group_check=True), inc=False)
                        top(lambda e, b_=eb: e.matmul(
                            PSL2[0:64, :], ONE[0:64, 0:64], b_[0:64, 0:512],
                            start=ste, stop=stp, skip_group_check=True), inc=False)
                        tpv_ = top(lambda e, b_=eb: e.matmul(
                            PSL2[64:128, :], ONE[0:64, 0:64], b_[0:64, 512:1024],
                            start=ste, stop=stp, skip_group_check=True))
                        last_pv_et[par] = tpv_
                        return tpv_

                    for up in range(NG // 2):
                        emit_S_exp(up)
                        last_exp_all = pv_args[up][3]
                        if up >= 1:
                            emit_PVl(up - 1)
                    tpv = emit_PVl(NG // 2 - 1)
                    if STOP in ('C2', 'C3'):
                        continue
                    ctx.wait('v', 'T', tpv)
                    vdrain()
                    vop(lambda e: e.tensor_copy(sgf[:, :], PSL1[:, :]))
                    vop(lambda e: e.tensor_copy(osum[:, :], PSO1[:, :]))
                    vdrain()
                    vop(lambda e: e.tensor_tensor(sgf[:, :], PSL2[:, :], sgf[:, :],
                                                  OP.add))
                    vop(lambda e: e.tensor_tensor(osum[:, :], PSO2[:, :], osum[:, :],
                                                  OP.add))
                    ep_psum_free = c['V']
                    vdrain()
                    vop(lambda e: e.reciprocal_approx_fast(lrec[:, :], sgf[:, :]))
                    vdrain()
                    ep_free_v = vop(lambda e, h_=hp, q_=q0: e.tensor_tensor(
                        o_sb[:, h_, q_:q_ + 512], osum[:, :], lrec[:, :], OP.mult))

            wo_l = dma_w(wo_sb[:, :, :], wob.rearrange("(a p) n -> p a n", p=128))
            ctx.wait('t', 'W', wo_l)
            ctx.wait('t', 'V', ep_free_v)
            ctx.wait('t', 'S', last_exp_all)
            xq_evac = []
            xmid_store = {}
            vxm_hist = {}
            trD_T = {}
            evD = {}
            qD = {}
            ctx.wait('s', 'T', c['T'])
            ctx.on('s', lambda e: e.dma_start(
                out=wg_sb[:, 0:2, :],
                in_=wgqb.rearrange("(a p) n -> p a n", p=128)[:, 0:2, :]),
                inc='C', n=16)
            ctx.on('s', lambda e: e.dma_start(
                out=wu_sb[:, 5:8, :],
                in_=wuqb.rearrange("(a p) n -> p a n", p=128)[:, 5:8, :]),
                inc='C', n=16)

            def emit_oprojD(t_):
                psu = [PSA, PSB][t_ % 2]
                if t_ >= 2:
                    ctx.wait('t', 'V', vxm_hist[t_ - 2])
                for n in range(2):
                    for ic in range(8):
                        top(lambda e, p_=psu, n_=n, i_=ic, tt_=t_: e.matmul(
                            p_[:, n_ * 512:(n_ + 1) * 512],
                            o_sb[:, i_, tt_ * 128:(tt_ + 1) * 128],
                            wo_sb[:, i_, n_ * 512:(n_ + 1) * 512],
                            start=(i_ == 0), stop=False), inc=False)
                    top(lambda e, p_=psu, n_=n: e.matmul(
                        p_[:, n_ * 512:(n_ + 1) * 512], ONE[0:1, 0:128],
                        BOB[0:1, n_ * 512:(n_ + 1) * 512],
                        start=False, stop=True), inc=(n == 1))
                return c['T']

            def emit_trD(t_):
                ctx.wait('t', 'V', qD[t_].c_mul)
                if t_ >= 1:
                    ctx.wait('t', 'S', evD[t_ - 1])
                for j in range(8):
                    psl = [PSL1, PSL2][j // 4]
                    ttranspose(psl[:, (j % 4) * 128:((j % 4) + 1) * 128],
                               xf[t_ % 2][:, j * 128:(j + 1) * 128], inc=(j == 7))
                trD_T[t_] = c['T']

            def emit_evacD(t_):
                ctx.wait('s', 'T', trD_T[t_])
                ctx.on('s', lambda e, t__=t_: e.activation(
                    xq_fm[:, 0:4, t__ * 128:(t__ + 1) * 128],
                    PSL1[:, :].rearrange("p (a n) -> p a n", n=128), ACT.Copy),
                    inc=None, n=1)
                ctx.on('s', lambda e, t__=t_: e.activation(
                    xq_fm[:, 4:8, t__ * 128:(t__ + 1) * 128],
                    PSL2[:, :].rearrange("p (a n) -> p a n", n=128), ACT.Copy),
                    inc='S', n=1)
                evD[t_] = c['S']

            oproj_T_last = 0
            for t in range(NT):
                b = t % 2
                psu = [PSA, PSB][t % 2]
                tD = emit_oprojD(t)
                oproj_T_last = tD
                g_eng, g_val = xf2_guard[b]
                if g_eng == 'G':
                    ctx.waitd('y', 'G')
                else:
                    ctx.wait('y', g_eng, g_val)
                dma_y(xf2[b][:, :], x_in[xbase + t * 128: xbase + (t + 1) * 128, :],
                      sem='M%d' % b)
                ctx.wait('v', 'T', tD)
                ctx.waitd('v', 'M%d' % b)
                vxm = vop(lambda e, p_=psu, bb=b: e.tensor_tensor(
                    xf2[bb][:, :], p_[:, :], xf2[bb][:, :], OP.add))
                vxm_hist[t] = vxm
                ctx.wait('g', 'V', vxm)
                xmid_store[t] = dma_g(xmid_hbm[t * 128:(t + 1) * 128, :], xf2[b][:, :])
                xf2_guard[b] = ('G', xmid_store[t])
                ctx.wait('v', 'V', vxm)
                vop(lambda e, bb=b: e.bn_stats(bnb[:, 0:6], xf2[bb][:, 0:512]), inc=False)
                vbn1 = vop(lambda e, bb=b: e.bn_stats(bnb[:, 6:12], xf2[bb][:, 512:1024]))
                if t >= 1:
                    qD[t - 1].s_add()
                ctx.wait('v', 'V', vbn1)
                vbn2 = vop(lambda e: e.bn_aggr(mv[:, 0:2], bnb[:, 0:12]))
                if t >= 1:
                    qD[t - 1].s_mult(dst_war=trD_T.get(t - 3, 0))
                    emit_trD(t - 1)
                ctx.wait('s', 'V', vbn2)
                sop(lambda e: e.activation(lnv[:, 0:1], mv[:, 1:2], ACT.Ln,
                                           bias=epsb[:, 0:1]))
                ctx.wait('s', 'S', c['S'])
                srs2 = sop(lambda e: e.activation(rstd[:, 0:1], lnv[:, 0:1], ACT.Exp,
                                                  scale=-0.5))
                if t >= 1:
                    emit_evacD(t - 1)
                ctx.wait('v', 'S', srs2)
                if t >= 1:
                    ctx.wait('v', 'V', qD[t - 1].c_ops)
                vap2 = vop(lambda e, bb=b: e.tensor_scalar(
                    xn2f[:, :], xf2[bb][:, :], mv[:, 0:1], rstd[:, 0:1],
                    OP.subtract, OP.mult))
                g_eng, g_val = xf_guard[b]
                qw = qD[t - 2].c_mul if t >= 2 else 0
                q = Quant(xf[b], xn2f, 1024, war=qw)
                if g_eng == 'V':
                    pass
                q.s_reduce(src_done=vap2)
                q.s_scale()
                q.s_recip()
                q.s_ops()
                qD[t] = q
            qD[NT - 1].s_add()
            qD[NT - 1].s_mult(dst_war=trD_T.get(NT - 3, 0))
            emit_trD(NT - 1)
            emit_evacD(NT - 1)
            for b in range(2):
                xf_guard[b] = ('T', trD_T[NT - 1 if (NT - 1) % 2 == b else NT - 2])

            wd_l = dma_w(wd_sb[:, :, :], wdqb.rearrange("(a p) n -> p a n", p=128))
            ctx.wait('s', 'T', oproj_T_last)
            wg_l = dma_w(wg_sb[:, 2:8, :], wgqb.rearrange("(a p) n -> p a n", p=128)[:, 2:8, :])
            wu_l = dma_w(wu_sb[:, 0:5, :], wuqb.rearrange("(a p) n -> p a n", p=128)[:, 0:5, :])

            ctx.wait('t', 'W', wu_l)
            ctx.waitd('t', 'C')
            ctx.wait('t', 'S', evD[NT - 1])
            PAIR_W = [1024, 1024, 768]
            IC_W = [512, 512, 512, 512, 512, 256]
            NPAIR = NT * 3
            qF = {}
            hm1 = {}
            trF = {}
            evF = {}
            dnF = {}
            vo2_hist = {}

            def emit_gateupF(pi):
                t_, pc_ = divmod(pi, 3)
                ctx.wait('t', 'V', hm1.get(pi - 1, 0))
                res = []
                for half in (0, 1):
                    ic = 2 * pc_ + half
                    ww = IC_W[ic]
                    psu = [PSA, PSB][half]
                    for wsb in (wg_sb, wu_sb):
                        for hc in range(8):
                            top(lambda e, p_=psu, c_=hc, w_=wsb, i_=ic, ww_=ww,
                                g_=(0 if wsb is wg_sb else 1), tt_=t_: e.matmul(
                                    p_[:, g_ * 512:g_ * 512 + ww_],
                                    xq_fm[:, c_, tt_ * 128:(tt_ + 1) * 128],
                                    w_[:, c_, i_ * 512:i_ * 512 + ww_],
                                    start=(c_ == 0), stop=(c_ == 7)),
                                inc=(hc == 7))
                    res.append(c['T'])
                return res

            def emit_hmults(pi, tgu):
                t_, pc_ = divmod(pi, 3)
                base = 0
                last = 0
                if pi == 0:
                    ctx.wait('s', 'V', ep_free_v)
                for half in (0, 1):
                    ic = 2 * pc_ + half
                    ww = IC_W[ic]
                    psu = [PSA, PSB][half]
                    ctx.wait('s', 'T', tgu[half])
                    if half == 1:
                        ctx.wait('s', 'V', c['V'])
                    ssg = sop(lambda e, p_=psu, ww_=ww: e.activation(
                        sgf[:, 0:ww_], p_[:, 0:ww_], ACT.Silu))
                    ctx.wait('v', 'S', ssg)
                    if half == 0 and pi >= 1:
                        ctx.wait('v', 'V', qF[pi - 1].c_ops)
                    last = vop(lambda e, p_=psu, ww_=ww, b_=base: e.tensor_tensor(
                        xn2f[:, b_:b_ + ww_], sgf[:, 0:ww_], p_[:, 512:512 + ww_],
                        OP.mult))
                    base += ww
                hm1[pi] = last
                return last

            def emit_evacF(pi):
                t_, pc_ = divmod(pi, 3)
                nsub = PAIR_W[pc_] // 128
                ctx.wait('v', 'T', trF[pi])
                ctx.wait('v', 'T', dnF.get(pi - 1, 0))
                n1 = min(nsub, 4)
                vop(lambda e, n_=n1: e.tensor_copy(hq_fm[:, 0:n_ * 128],
                                                   PSL1[:, 0:n_ * 128]),
                    inc=(nsub <= 4))
                if nsub > 4:
                    vop(lambda e, n_=nsub - 4: e.tensor_copy(
                        hq_fm[:, 512:512 + n_ * 128], PSL2[:, 0:n_ * 128]))
                evF[pi] = c['V']

            def emit_trF(pi):
                t_, pc_ = divmod(pi, 3)
                nsub = PAIR_W[pc_] // 128
                ctx.wait('t', 'V', qF[pi].c_mul)
                ctx.wait('t', 'V', evF.get(pi - 1, 0))
                for j in range(nsub):
                    psl = [PSL1, PSL2][j // 4]
                    ttranspose(psl[:, (j % 4) * 128:((j % 4) + 1) * 128],
                               xf2[pi % 2][:, j * 128:(j + 1) * 128],
                               inc=(j == nsub - 1))
                trF[pi] = c['T']

            def emit_downF(pi):
                t_, pc_ = divmod(pi, 3)
                nsub = PAIR_W[pc_] // 128
                ctx.wait('t', 'V', evF[pi])
                if pc_ == 0:
                    ctx.wait('t', 'V', vo2_hist.get(t_ - 1, 0))
                for j in range(nsub):
                    isub = 8 * pc_ + j
                    st_ = (isub == 0)
                    sp_ = (isub == 21)
                    top(lambda e, j_=j, i_=isub, st__=st_, sp__=sp_: e.matmul(
                        PSO1[:, :], hq_fm[:, j_ * 128:(j_ + 1) * 128],
                        wd_sb[:, i_, 0:512], start=st__, stop=sp__), inc=False)
                    top(lambda e, j_=j, i_=isub, st__=st_, sp__=sp_: e.matmul(
                        PSO2[:, :], hq_fm[:, j_ * 128:(j_ + 1) * 128],
                        wd_sb[:, i_, 512:1024], start=st__, stop=sp__),
                        inc=(j == nsub - 1))
                dnF[pi] = c['T']

            def emit_residF_load(t_):
                b = t_ % 2
                g_eng, g_val = xf_guard[b]
                if g_eng == 'G':
                    ctx.waitd('y', 'G')
                else:
                    ctx.wait('y', g_eng, g_val)
                ctx.waitd('y', 'G')
                dma_y(xf[b][:, :], xmid_hbm[t_ * 128:(t_ + 1) * 128, :],
                      sem='L%d' % b)

            def emit_residF(t_):
                b = t_ % 2
                tdn = dnF[3 * t_ + 2]
                ctx.wait('v', 'T', tdn)
                ctx.waitd('v', 'L%d' % b)
                vop(lambda e, b_=b: e.tensor_tensor(
                    xf[b_][:, 0:512], PSO1[:, :], xf[b_][:, 0:512], OP.add))
                vo2 = vop(lambda e, b_=b: e.tensor_tensor(
                    xf[b_][:, 512:1024], PSO2[:, :], xf[b_][:, 512:1024], OP.add))
                vo2_hist[t_] = vo2
                ctx.wait('g', 'V', vo2)
                gst = dma_g(y_out[xbase + t_ * 128: xbase + (t_ + 1) * 128, :],
                            xf[b][:, :])
                xf_guard[b] = ('G', gst)

            for pi in range(NPAIR):
                t_, pc_ = divmod(pi, 3)
                tgu = emit_gateupF(pi)
                if pi >= 1:
                    qF[pi - 1].s_add()
                    if pi <= 2:
                        ctx.waitd('v', 'G')
                    qF[pi - 1].s_mult(dst_war=trF.get(pi - 3, 0))
                    emit_trF(pi - 1)
                if pc_ == 0 and t_ >= 1:
                    emit_residF_load(t_ - 1)
                emit_hmults(pi, tgu)
                q = Quant(xf2[pi % 2], xn2f, PAIR_W[pc_])
                qF[pi] = q
                if pi >= 1:
                    emit_evacF(pi - 1)
                    if (pi - 1) % 3 == 0 and t_ >= 1:
                        emit_residF(t_ - 1)
                    emit_downF(pi - 1)
                q.s_reduce(src_done=hm1[pi])
                q.s_scale()
                q.s_recip()
                q.s_ops()
            pi = NPAIR - 1
            qF[pi].s_add()
            qF[pi].s_mult(dst_war=trF.get(pi - 2, 0))
            emit_trF(pi)
            emit_residF_load(NT - 1)
            emit_evacF(pi)
            emit_downF(pi)
            emit_residF(NT - 1)
            for b in range(2):
                last_pi = NPAIR - 1 if (NPAIR - 1) % 2 == b else NPAIR - 2
                xf2_guard[b] = ('T', trF[last_pi])
            prev_seq_V_end = c['V']



        ctx.on('g', lambda e: e.wait_ge(ctx.sem['G'], c['G']))

    with (
        nc.semaphore("semL0") as semL0,
        nc.semaphore("semL1") as semL1,
        nc.semaphore("semM0") as semM0,
        nc.semaphore("semM1") as semM1,
        nc.semaphore("semC") as semC,
        nc.semaphore("semW") as semW,
        nc.semaphore("semG") as semG,
        nc.semaphore("semT") as semT,
        nc.semaphore("semV") as semV,
        nc.semaphore("semS") as semS,
        nc.Block() as block,
    ):
        sems = {"L0": semL0, "L1": semL1, "M0": semM0, "M1": semM1, "C": semC,
                "W": semW, "G": semG, "T": semT, "V": semV, "S": semS}

        @block.sync
        def _(eng):
            emit(Ctx('y', eng, sems))

        @block.vector
        def _(eng):
            emit(Ctx('v', eng, sems))

        @block.scalar
        def _(eng):
            emit(Ctx('s', eng, sems))

        @block.tensor
        def _(eng):
            emit(Ctx('t', eng, sems))

        @block.gpsimd
        def _(eng):
            emit(Ctx('g', eng, sems))

    mybir.codegen_inst_isa_subclasses(nc)
    return nc


_NC_CACHE = {}


def kernel(L0_attn_hidden_states, L0_attn_Wq, L0_attn_Wkv, L0_attn_Wo, L0_attn_bo,
           L0_ffn_w_gate, L0_ffn_w_up, L0_ffn_w_down):
    x = np.ascontiguousarray(np.asarray(L0_attn_hidden_states, dtype=np.float32))
    B = x.shape[0]
    wq = np.ascontiguousarray(np.asarray(L0_attn_Wq, dtype=np.float32))
    wkv = np.asarray(L0_attn_Wkv, dtype=np.float32)
    wk = np.ascontiguousarray(wkv[:, :D])
    wv = np.ascontiguousarray(wkv[:, D:])
    wo = np.ascontiguousarray(np.asarray(L0_attn_Wo, dtype=np.float32))
    bo = np.ascontiguousarray(np.asarray(L0_attn_bo, dtype=np.float32)).reshape(1, D)
    wg = np.ascontiguousarray(np.asarray(L0_ffn_w_gate, dtype=np.float32))
    wu = np.ascontiguousarray(np.asarray(L0_ffn_w_up, dtype=np.float32))
    wd = np.ascontiguousarray(np.asarray(L0_ffn_w_down, dtype=np.float32))
    ident = np.eye(128, dtype=np.float32)
    onesb = np.ones((128, 128), dtype=ml_dtypes.bfloat16)

    if "nc" not in _NC_CACHE:
        _NC_CACHE["nc"] = build_program()
    nc = _NC_CACHE["nc"]

    n_cores = 8
    per = B // n_cores
    in_maps = []
    for ci in range(n_cores):
        shard = np.ascontiguousarray(x[ci * per:(ci + 1) * per].reshape(per * S, D))
        in_maps.append({
            "x": shard, "wq": wq, "wk": wk, "wv": wv, "wo": wo, "bo": bo,
            "wg": wg, "wu": wu, "wd": wd, "ident": ident, "onesb": onesb,
        })
    res = run_bass_kernel_spmd(nc, in_maps, core_ids=list(range(n_cores)))
    return np.concatenate([r["y"].reshape(per, S, D) for r in res.results], axis=0)


# version 10 (final)
# speedup vs baseline: 1.2353x; 1.0093x over previous
import numpy as np
import ml_dtypes

import concourse.bass as bass
import concourse.mybir as mybir
from concourse.bass_utils import run_bass_kernel_spmd
from concourse.dve_spec import Spec, Src0, Src1, C0, C1, Zero, One, maxx, minn, lower
from concourse.dve_op import DveOp
import concourse.dve_ops as dve_ops

F32 = mybir.dt.float32
BF16 = mybir.dt.bfloat16
AX = mybir.AxisListType
OP = mybir.AluOpType
ACT = mybir.ActivationFunctionType

C_VELT = float(2 ** 22 + 1)
C_MAGIC = float(0.75 * 2 ** 23)
_GRID = np.array([0.0, 0.5, 1.0, 1.5, 2.0, 3.0, 4.0, 6.0])
_MID = np.array([0.25, 0.75, 1.25, 1.75, 2.5, 3.5, 5.0])


def _fp4_ref(in0, in1, s0, s1, imm2, part):
    f = np.float64 if in0.dtype == np.float64 else np.float32
    xn = in0.astype(f) * in1.astype(f)
    if f is np.float64:
        idx = np.searchsorted(_MID, np.abs(xn).ravel()).reshape(np.shape(xn))
        r = _GRID[idx] * np.sign(xn)
    else:
        xn = xn.astype(np.float32)
        c = (xn * np.float32(C_VELT)).astype(np.float32)
        d = (c - xn).astype(np.float32)
        hi = (c - d).astype(np.float32)
        l = ((xn + np.float32(C_MAGIC)) - np.float32(C_MAGIC)).astype(np.float32)
        r = (hi - np.clip(hi, -2, 2)) + np.clip(l, -2, 2)
    out = (r - np.clip(r, -2.0, 2.0)) if part == "ovf" else np.clip(r, -2.0, 2.0)
    return out.astype(f)


def _mk_bodyA():
    xn = Src0 * Src1
    c = xn * C1
    d = c - xn
    hi = c - d
    m = minn(maxx(hi, Zero - (One + One)), One + One)
    return hi - m


def _mk_bodyB():
    xn = Src0 * Src1
    l = (xn + C0) - C0
    return minn(maxx(l, Zero - (One + One)), One + One)


def _mk_bodyS():
    return maxx(Src0, C0) * C1


def _scale6_ref(in0, in1, s0, s1, imm2):
    if in0.dtype == np.float64:
        return np.maximum(in0, np.float64(np.float32(6e-12))) / 6.0
    return (np.maximum(in0.astype(np.float32), np.float32(s0)) *
            np.float32(s1)).astype(np.float32)


def _register_fp4_ops():
    have = {op.name: op for op in dve_ops.OPS}
    if "FP4OVF" in have:
        return have["FP4OVF"], have["FP4LOW"], have["FP4SCALE"]
    shaA = {v: lower(Spec(body=_mk_bodyA()), ver=v, perf_en=False, name="FP4OVF").sha(v)
            for v in ("v3", "v4")}
    shaB = {v: lower(Spec(body=_mk_bodyB()), ver=v, perf_en=False, name="FP4LOW").sha(v)
            for v in ("v3", "v4")}
    opA = DveOp("FP4OVF", Spec(body=_mk_bodyA()), subdim=False, uops_sha=shaA,
                reference=lambda i0, i1, s0, s1, imm2: _fp4_ref(i0, i1, s0, s1, imm2, "ovf"))
    opB = DveOp("FP4LOW", Spec(body=_mk_bodyB()), subdim=False, uops_sha=shaB,
                reference=lambda i0, i1, s0, s1, imm2: _fp4_ref(i0, i1, s0, s1, imm2, "low"))
    shaS = {v: lower(Spec(body=_mk_bodyS()), ver=v, perf_en=False, name="FP4SCALE").sha(v)
            for v in ("v3", "v4")}
    opS = DveOp("FP4SCALE", Spec(body=_mk_bodyS()), subdim=False, uops_sha=shaS,
                reference=_scale6_ref)
    dve_ops.OPS.append(opA)
    dve_ops.OPS.append(opB)
    dve_ops.OPS.append(opS)
    dve_ops._SUB_OPCODE_FOR_NAME["FP4OVF"] = 31
    dve_ops._SUB_OPCODE_FOR_NAME["FP4LOW"] = 30
    dve_ops._SUB_OPCODE_FOR_NAME["FP4SCALE"] = 29
    for o in (opA, opB, opS):
        dve_ops.CUSTOM_DVE_SPECS[o.name] = o.spec
        dve_ops.CUSTOM_DVE_REFS[o.name] = o.ref
    return opA, opB, opS


OPA, OPB, OPS6 = _register_fp4_ops()

NSEQ = 4
STOP = 'Z'
S = 2048
D = 1024
NH = 16
I = 2816
NT = S // 128
NHP = NH // 2
NG = S // 64
AM_FLOOR = float(np.float32(6e-12))
FLOOR12 = float(np.float32(1e-12))
C6 = float(np.float32(1.0) / np.float32(6.0))


class Ctx:

    def __init__(self, active, engine, sems):
        self.a = active
        self.eng = engine
        self.sem = sems
        self.c = {k: 0 for k in sems}

    def on(self, eng, fn, inc=None, n=1):
        if self.a == eng:
            inst = fn(self.eng)
            if inc is not None:
                inst.then_inc(self.sem[inc], n)
        if inc is not None:
            self.c[inc] += n
            return self.c[inc]
        return None

    def wait(self, eng, sem, val):
        if val and val > 0 and self.a == eng:
            self.eng.wait_ge(self.sem[sem], val)

    def waitd(self, eng, sem):
        self.wait(eng, sem, self.c[sem])


def build_program():
    nc = bass.Bass("TRN2", target_bir_lowering=False, debug=False)

    x_in = nc.dram_tensor("x", [NSEQ * S, D], F32, kind="ExternalInput").ap()
    wq_in = nc.dram_tensor("wq", [D, D], F32, kind="ExternalInput").ap()
    wk_in = nc.dram_tensor("wk", [D, D], F32, kind="ExternalInput").ap()
    wv_in = nc.dram_tensor("wv", [D, D], F32, kind="ExternalInput").ap()
    wo_in = nc.dram_tensor("wo", [D, D], F32, kind="ExternalInput").ap()
    bo_in = nc.dram_tensor("bo", [1, D], F32, kind="ExternalInput").ap()
    wg_in = nc.dram_tensor("wg", [I, D], F32, kind="ExternalInput").ap()
    wu_in = nc.dram_tensor("wu", [I, D], F32, kind="ExternalInput").ap()
    wd_in = nc.dram_tensor("wd", [D, I], F32, kind="ExternalInput").ap()
    ident_in = nc.dram_tensor("ident", [128, 128], F32, kind="ExternalInput").ap()
    ones_in = nc.dram_tensor("onesb", [128, 128], BF16, kind="ExternalInput").ap()
    y_out = nc.dram_tensor("y", [NSEQ * S, D], F32, kind="ExternalOutput").ap()

    wqb = nc.dram_tensor("wqb", [D, D], BF16, kind="Internal").ap()
    wkb = nc.dram_tensor("wkb", [D, D], BF16, kind="Internal").ap()
    wvb = nc.dram_tensor("wvb", [D, D], BF16, kind="Internal").ap()
    wob = nc.dram_tensor("wob", [D, D], BF16, kind="Internal").ap()
    wgqb = nc.dram_tensor("wgqb", [D, I], BF16, kind="Internal").ap()
    wuqb = nc.dram_tensor("wuqb", [D, I], BF16, kind="Internal").ap()
    wdqb = nc.dram_tensor("wdqb", [I, D], BF16, kind="Internal").ap()
    xmid_hbm = nc.dram_tensor("xmidh", [S, D], F32, kind="Internal").ap()

    abf = nc.alloc_sbuf_tensor("abf", [128, 88 * 1024], BF16)
    af = nc.alloc_sbuf_tensor("af", [128, 7424], F32)
    ident = nc.alloc_sbuf_tensor("identsb", [128, 128], F32)
    onesb = nc.alloc_sbuf_tensor("onessb", [128, 128], BF16)
    bob = nc.alloc_sbuf_tensor("bobsb", [1, 1024], BF16)
    AB, AF = abf.ap(), af.ap()

    def bfv(off, shape):
        n = int(np.prod(shape))
        v = AB[:, off:off + n]
        if len(shape) > 1:
            v = v.rearrange("p (a b) -> p a b", b=shape[-1])
        return v

    def fv(off, shape):
        n = int(np.prod(shape))
        v = AF[:, off:off + n]
        if len(shape) > 1:
            v = v.rearrange("p (a b) -> p a b", b=shape[-1])
        return v

    K_ = 1024
    xn_fm = bfv(0, [8, S])
    xq_fm = xn_fm
    v_sb = bfv(16 * K_, [NT, 1024])
    wd_sb = bfv(16 * K_, [22, 1024])
    q_sb = [bfv(32 * K_ + i * 4 * K_, [S]) for i in range(2)]
    k_sb = [bfv(40 * K_ + i * 4 * K_, [S]) for i in range(2)]
    o_sb = bfv(48 * K_, [8, S])
    wg_sb = bfv(40 * K_, [8, I])
    et = [bfv(64 * K_ + i * K_, [1024]) for i in range(2)]
    wq_sb = bfv(66 * K_, [8, 1024])
    wo_sb = wq_sb
    wk_sb = bfv(74 * K_, [8, 1024])
    wu_sb = bfv(62 * K_ + 512, [8, I])
    stg_b = [bfv(85 * K_ + i * K_, [1024]) for i in range(2)]
    qt_ovf = bfv(85 * K_, [1024])
    qt_low = bfv(86 * K_, [1024])
    qt_shb = bfv(87 * K_, [176])
    hq_fm = bfv(38 * K_, [1024])
    wfm = bfv(24 * K_, [8, I])
    wfm_d = bfv(24 * K_, [22, 1024])
    sq_ovf = bfv(48 * K_, [I])
    sq_low = bfv(51 * K_, [I])
    sq_shb = bfv(54 * K_, [176])

    xf = [fv(i * K_, [1024]) for i in range(2)]
    xf2 = [fv(2 * K_ + i * K_, [1024]) for i in range(2)]
    xn2f = fv(4 * K_, [1024])
    lrec = fv(5 * K_, [512])
    osum = fv(5 * K_ + 512, [512])
    sgf = fv(6 * K_, [512])
    wdstg = [fv(0, [I]), fv(3 * K_, [I])]
    bnb = fv(6656, [12])
    mv = fv(6672, [2])
    lnv = fv(6676, [1])
    rstd = fv(6680, [1])
    am_t = fv(6688, [176])
    epsb = fv(7232, [1])
    sh_t = fv(6864, [176])
    rc_t = fv(7040, [176])

    ps_a = nc.alloc_psum_tensor("ps_a", [128, 1024], F32)
    ps_b = nc.alloc_psum_tensor("ps_b", [128, 1024], F32)
    ps_o1 = nc.alloc_psum_tensor("ps_o1", [128, 512], F32)
    ps_o2 = nc.alloc_psum_tensor("ps_o2", [128, 512], F32)
    ps_l1 = nc.alloc_psum_tensor("ps_l1", [128, 512], F32)
    ps_l2 = nc.alloc_psum_tensor("ps_l2", [128, 512], F32)
    PSA, PSB = ps_a.ap(), ps_b.ap()
    PSO1, PSO2, PSL1, PSL2 = ps_o1.ap(), ps_o2.ap(), ps_l1.ap(), ps_l2.ap()
    IDT = ident.ap()
    ONE = onesb.ap()
    BOB = bob.ap()

    def emit(ctx: Ctx):
        c = ctx.c

        def dma_y(dst, src, sem='C'):
            return ctx.on('y', lambda e: e.dma_start(out=dst, in_=src), inc=sem, n=16)

        def dma_w(dst, src):
            return ctx.on('s', lambda e: e.dma_start(out=dst, in_=src), inc='W', n=16)

        def dma_g(dst, src):
            return ctx.on('g', lambda e: e.dma_start(out=dst, in_=src), inc='G', n=16)

        def vop(fn, inc=True):
            return ctx.on('v', fn, inc='V' if inc else None, n=1)

        def sop(fn, inc=True):
            return ctx.on('s', fn, inc='S' if inc else None, n=1)

        def top(fn, inc=True):
            return ctx.on('t', fn, inc='T' if inc else None, n=1)

        def vdrain():
            ctx.wait('v', 'V', c['V'])

        def ttranspose(out_ap, in_ap, inc=False):
            return top(lambda e: e.matmul(out_ap, in_ap, IDT[:, :],
                                          is_transpose=True, skip_group_check=True),
                       inc=inc)

        class Quant:

            def __init__(self, dst, src_f32, width, scr=None, war=0):
                self.dst = dst
                self.src = src_f32[:, 0:width].rearrange("p (b k) -> p b k", k=16)
                self.w = width
                self.nb = width // 16
                self.scr = scr if scr is not None else (qt_ovf, qt_low, qt_shb)
                self.war = war

            def s_reduce(self, src_done=0):
                ctx.wait('v', 'V', max(self.war, src_done))
                self.c_red = vop(lambda e: e.tensor_reduce(
                    am_t[:, 0:self.nb], self.src, axis=AX.X, op=OP.max,
                    apply_absolute_value=True))

            def s_scale(self):
                ctx.wait('v', 'V', self.c_red)
                self.c_sc = vop(lambda e: e._custom_dve(
                    OPS6, out=sh_t[:, 0:self.nb], in0=am_t[:, 0:self.nb],
                    s0=AM_FLOOR, s1=C6))

            def s_recip(self):
                ctx.wait('v', 'V', self.c_sc)
                vop(lambda e: e.tensor_copy(self.scr[2][:, 0:self.nb],
                                            sh_t[:, 0:self.nb]), inc=False)
                self.c_rc = vop(lambda e: e.reciprocal_approx_fast(
                    rc_t[:, 0:self.nb], sh_t[:, 0:self.nb]))

            def s_ops(self):
                ctx.wait('v', 'V', self.c_rc)
                rcb = rc_t[:, 0:self.nb].broadcast_to([128, self.nb, 16])
                vop(lambda e, r_=rcb: e._custom_dve(
                    OPA, out=self.scr[0][:, 0:self.w].rearrange("p (b k) -> p b k", k=16),
                    in0=self.src, in1=r_, s0=0.0, s1=C_VELT), inc=False)
                self.c_ops = vop(lambda e, r_=rcb: e._custom_dve(
                    OPB, out=self.scr[1][:, 0:self.w].rearrange("p (b k) -> p b k", k=16),
                    in0=self.src, in1=r_, s0=C_MAGIC, s1=0.0))

            def s_add(self):
                ctx.wait('v', 'V', self.c_ops)
                self.c_add = vop(lambda e: e.tensor_tensor(
                    self.scr[0][:, 0:self.w], self.scr[0][:, 0:self.w],
                    self.scr[1][:, 0:self.w], OP.add))

            def s_mult(self, dst_war=0):
                ctx.wait('v', 'V', self.c_add)
                if dst_war:
                    ctx.wait('v', 'T', dst_war)
                shb = self.scr[2][:, 0:self.nb].broadcast_to([128, self.nb, 16])
                self.c_mul = vop(lambda e: e.tensor_tensor(
                    self.dst[:, 0:self.w].rearrange("p (b k) -> p b k", k=16),
                    self.scr[0][:, 0:self.w].rearrange("p (b k) -> p b k", k=16),
                    shb, OP.mult))
                return self.c_mul

        def quant_tile(dst_bf, src_f32, width, scr=None, war=0, src_done=0,
                       dst_war=0):
            q = Quant(dst_bf, src_f32, width, scr=scr, war=war)
            q.s_reduce(src_done)
            q.s_scale()
            q.s_recip()
            q.s_ops()
            q.s_add()
            return q.s_mult(dst_war)

        def ttranspose(out_ap, in_ap, inc=False):
            return top(lambda e: e.matmul(out_ap, in_ap, IDT[:, :],
                                          is_transpose=True, skip_group_check=True),
                       inc=inc)

        def quant_tile(dst_bf, src_f32, width, scr=None):
            t_ovf, t_low, t_shb = scr if scr is not None else (qt_ovf, qt_low, qt_shb)
            nb = width // 16
            sv = src_f32[:, 0:width].rearrange("p (b k) -> p b k", k=16)
            vdrain()
            vop(lambda e: e.tensor_reduce(am_t[:, 0:nb], sv, axis=AX.X, op=OP.max,
                                          apply_absolute_value=True))
            vdrain()
            vop(lambda e: e._custom_dve(OPS6, out=sh_t[:, 0:nb], in0=am_t[:, 0:nb],
                                        s0=AM_FLOOR, s1=C6))
            vdrain()
            vop(lambda e: e.tensor_copy(t_shb[:, 0:nb], sh_t[:, 0:nb]))
            vop(lambda e: e.reciprocal_approx_fast(rc_t[:, 0:nb], sh_t[:, 0:nb]))
            vdrain()
            rcb = rc_t[:, 0:nb].broadcast_to([128, nb, 16])
            shb = t_shb[:, 0:nb].broadcast_to([128, nb, 16])
            vop(lambda e: e._custom_dve(
                OPA, out=t_ovf[:, 0:width].rearrange("p (b k) -> p b k", k=16),
                in0=sv, in1=rcb, s0=0.0, s1=C_VELT))
            vop(lambda e: e._custom_dve(
                OPB, out=t_low[:, 0:width].rearrange("p (b k) -> p b k", k=16),
                in0=sv, in1=rcb, s0=C_MAGIC, s1=0.0))
            vdrain()
            vop(lambda e: e.tensor_tensor(t_ovf[:, 0:width], t_ovf[:, 0:width],
                                          t_low[:, 0:width], OP.add))
            vdrain()
            return vop(lambda e: e.tensor_tensor(
                dst_bf[:, 0:width].rearrange("p (b k) -> p b k", k=16),
                t_ovf[:, 0:width].rearrange("p (b k) -> p b k", k=16),
                shb, OP.mult))

        ctx.on('v', lambda e: e.memset(epsb[:, 0:1], 1e-5))
        dma_y(IDT[:, :], ident_in[:, :])
        dma_y(ONE[:, :], ones_in[:, :])
        cl_bo = dma_y(xf[0][0:1, :], bo_in[0:1, :])
        ctx.waitd('v', 'C')
        v_bo = vop(lambda e: e.tensor_copy(BOB[0:1, :], xf[0][0:1, :]))

        ctx.waitd('t', 'C')

        cast_v = {}

        def attn_cast_load(i):
            b = i % 2
            wi, ch = divmod(i, 8)
            src = (wq_in, wk_in, wv_in, wo_in)[wi]
            ctx.wait('y', 'V', cast_v[i - 2] if i >= 2 else v_bo)
            dma_y(xf[b][:, :], src[ch * 128:(ch + 1) * 128, :], sem='L%d' % b)

        def attn_cast_do(i):
            b = i % 2
            wi, ch = divmod(i, 8)
            dst = (wqb, wkb, wvb, wob)[wi]
            ctx.waitd('v', 'L%d' % b)
            if i >= 2:
                ctx.waitd('v', 'C')
            cast_v[i] = vop(lambda e, b_=b: e.tensor_copy(stg_b[b_][:, :],
                                                          xf[b_][:, :]))
            ctx.wait('y', 'V', cast_v[i])
            ctx.on('y', lambda e, d_=dst, c_=ch, b_=b: e.dma_start(
                out=d_[c_ * 128:(c_ + 1) * 128, :], in_=stg_b[b_][:, :]),
                inc='C', n=16)

        CH = []
        for wsel, (src_w, dstb) in enumerate(((wg_in, wgqb), (wu_in, wuqb),
                                              (wd_in, wdqb))):
            for chq in range(8 if wsel == 2 else 22):
                CH.append((wsel, chq, src_w, dstb))
        NCH = len(CH)
        scrA = (sq_ovf, sq_low, sq_shb)
        scrB = (bfv(0, [I]), bfv(3 * K_, [I]), bfv(6 * K_, [176]))

        def stg_of(ci):
            wsel = CH[ci][0]
            return ((wdstg[ci % 2], I) if wsel == 2 else (xf2[ci % 2], 1024))

        tr_chunk = {}
        ev_group = {}
        g_state = [0]
        qS = {}

        def load_ch(ci):
            wsel, chq, src_w, dstb = CH[ci]
            dst, wid = stg_of(ci)
            if ci - 2 in tr_chunk:
                ctx.wait('s', 'T', tr_chunk[ci - 2])
            if ci in (44, 45):
                ctx.wait('s', 'V', cast_v[31])
            ctx.on('s', lambda e, d_=dst, w_=wid, s_=src_w, q_=chq: e.dma_start(
                out=d_[:, 0:w_], in_=s_[q_ * 128:(q_ + 1) * 128, :]),
                inc='M%d' % (ci % 2), n=16)

        def tr_evac_ch(ci):
            wsel, chq, src_w, dstb = CH[ci]
            dst, wid = stg_of(ci)
            ncol = wid // 128
            fm = wfm_d if wsel == 2 else wfm
            nch_w = 8 if wsel == 2 else 22
            ctx.wait('t', 'V', qS[ci].c_mul)
            for b0 in range(0, ncol, 8):
                nb8 = min(8, ncol - b0)
                g = g_state[0]
                psu = [PSA, PSB][g % 2]
                if g - 2 in ev_group:
                    ctx.wait('t', 'S', ev_group[g - 2])
                for j in range(nb8):
                    bb = b0 + j
                    ttranspose(psu[:, j * 128:(j + 1) * 128],
                               dst[:, bb * 128:(bb + 1) * 128],
                               inc=(j == nb8 - 1))
                ctx.wait('s', 'T', c['T'])
                if b0 == 0 and chq == 0:
                    ctx.waitd('s', 'G')
                ctx.on('s', lambda e, p_=psu, n_=nb8, f_=fm, b_=b0, q_=chq:
                       e.activation(f_[:, b_:b_ + n_, q_ * 128:(q_ + 1) * 128],
                                    p_[:, 0:n_ * 128].rearrange(
                                        "p (a n) -> p a n", n=128),
                                    ACT.Copy),
                       inc='S', n=1)
                ev_group[g] = c['S']
                g_state[0] = g + 1
            tr_chunk[ci] = c['T']
            if chq == nch_w - 1:
                ctx.wait('g', 'S', c['S'])
                dma_g(dstb.rearrange("(a p) n -> p a n", p=128), fm[:, :, :])

        attn_cast_load(0)
        attn_cast_load(1)
        load_ch(0)
        load_ch(1)
        for ci in range(NCH):
            if ci >= 1:
                qS[ci - 1].s_add()
                qS[ci - 1].s_mult()
                tr_evac_ch(ci - 1)
                if ci + 1 < NCH:
                    load_ch(ci + 1)
            if ci < 32:
                attn_cast_do(ci)
            if ci + 2 <= 31:
                attn_cast_load(ci + 2)
            dst, wid = stg_of(ci)
            ctx.waitd('v', 'M%d' % (ci % 2))
            q = Quant(dst, dst, wid, scr=(scrA if ci % 2 == 0 else scrB))
            qS[ci] = q
            q.s_reduce(src_done=(qS[ci - 1].c_mul if ci >= 1 else 0))
            q.s_scale()
            q.s_recip()
            q.s_ops()
        qS[NCH - 1].s_add()
        qS[NCH - 1].s_mult()
        tr_evac_ch(NCH - 1)
        last_attnw_cast = cast_v[31]
        setup_done_G = c['G']
        setup_done_V = c['V']
        setup_done_T = c['T']
        setup_done_S = c['S']

        if STOP == 'S':
            ctx.on('g', lambda e: e.wait_ge(ctx.sem['G'], c['G']))
            return
        ctx.wait('v', 'T', setup_done_T)
        ctx.wait('t', 'S', setup_done_S)
        xf_guard = [('T', tr_chunk[50]), ('T', tr_chunk[50])]
        xf2_guard = [('T', tr_chunk[50]), ('T', tr_chunk[51])]
        prev_seq_V_end = 0

        for s_i in range(NSEQ):
            xbase = s_i * S
            ctx.waitd('s', 'G')
            ctx.waitd('s', 'C')
            wq_l = dma_w(wq_sb[:, :, :], wqb.rearrange("(a p) n -> p a n", p=128))
            wk_l = dma_w(wk_sb[:, :, :], wkb.rearrange("(a p) n -> p a n", p=128))
            wv_l = dma_w(v_sb[:, 0, 0:1024] if False else bfv(48 * K_, [8, 1024])[:, :, :],
                         wvb.rearrange("(a p) n -> p a n", p=128))
            wv_sb = bfv(48 * K_, [8, 1024])

            evacA_S = {}
            trA_T = {}
            vapA = {}

            def loadA(t):
                b = t % 2
                g_eng, g_val = xf_guard[b]
                if g_eng == 'G':
                    ctx.waitd('y', 'G')
                else:
                    ctx.wait('y', g_eng, g_val)
                dma_y(xf[b][:, :], x_in[xbase + t * 128: xbase + (t + 1) * 128, :],
                      sem='L%d' % b)

            def trevA(t):
                ctx.wait('t', 'V', vapA[t])
                if t >= 2:
                    ctx.wait('t', 'S', evacA_S[t - 2])
                elif s_i > 0:
                    ctx.wait('t', 'V', prev_seq_V_end)
                psT = [PSA, PSB][t % 2]
                for j in range(8):
                    ttranspose(psT[:, j * 128:(j + 1) * 128],
                               xn2f[:, j * 128:(j + 1) * 128], inc=(j == 7))
                trA_T[t] = c['T']
                ctx.wait('s', 'T', c['T'])
                ctx.on('s', lambda e, t_=t, p_=psT: e.activation(
                    xn_fm[:, :, t_ * 128:(t_ + 1) * 128],
                    p_[:, :].rearrange("p (a n) -> p a n", n=128), ACT.Copy),
                    inc='S', n=1)
                evacA_S[t] = c['S']

            loadA(0)
            for t in range(NT):
                b = t % 2
                if t + 1 < NT:
                    loadA(t + 1)
                ctx.waitd('v', 'L%d' % b)
                vop(lambda e, b_=b: e.bn_stats(bnb[:, 0:6], xf[b_][:, 0:512]),
                    inc=False)
                vop(lambda e, b_=b: e.bn_stats(bnb[:, 6:12], xf[b_][:, 512:1024]))
                vdrain()
                vbn = vop(lambda e: e.bn_aggr(mv[:, 0:2], bnb[:, 0:12]))
                ctx.wait('s', 'V', vbn)
                sop(lambda e: e.activation(lnv[:, 0:1], mv[:, 1:2], ACT.Ln, bias=epsb[:, 0:1]))
                ctx.wait('s', 'S', c['S'])
                srs = sop(lambda e: e.activation(rstd[:, 0:1], lnv[:, 0:1], ACT.Exp,
                                                 scale=-0.5))
                if t >= 1:
                    trevA(t - 1)
                ctx.wait('v', 'S', srs)
                if t >= 1:
                    ctx.wait('v', 'T', trA_T[t - 1])
                vap = vop(lambda e, b_=b: e.tensor_scalar(
                    xn2f[:, :], xf[b_][:, :], mv[:, 0:1], rstd[:, 0:1],
                    OP.subtract, OP.mult))
                vapA[t] = vap
                xf_guard[b] = ('V', vap)
            trevA(NT - 1)

            if STOP == 'A':
                continue
            ctx.waitd('t', 'W')
            vprod_ev = []
            for t in range(NT):
                psu = [PSA, PSB][t % 2]
                ctx.wait('t', 'S', evacA_S[NT - 1])
                if t >= 2:
                    ctx.wait('t', 'V', vprod_ev[t - 2])
                for n in range(2):
                    for dc in range(8):
                        top(lambda e, p_=psu, n_=n, d_=dc, t_=t: e.matmul(
                            p_[:, n_ * 512:(n_ + 1) * 512],
                            xn_fm[:, d_, t_ * 128:(t_ + 1) * 128],
                            wv_sb[:, d_, n_ * 512:(n_ + 1) * 512],
                            start=(d_ == 0), stop=(d_ == 7)),
                            inc=(n == 1 and d_ == 7) if False else (n == 1 and dc == 7))
                ctx.wait('v', 'T', c['T'])
                vprod_ev.append(vop(lambda e, p_=psu, t_=t: e.tensor_copy(
                    v_sb[:, t_, :], p_[:, :])))

            if STOP == 'B':
                continue
            ctx.waitd('t', 'W')
            last_exp_bank = [0, 0]
            last_pv_et = [0, 0]
            ep_free_v = 0
            ep_psum_free = 0
            last_exp_all = 0
            qk_ready_hp = {}

            def qkprod(hp2, wi):
                pb2 = hp2 % 2
                wsb, dsb, psu, par = ((wq_sb, q_sb[pb2], PSA, 0),
                                     (wk_sb, k_sb[pb2], PSB, 1))[wi]
                ctx.wait('t', 'S', last_exp_bank[par])
                if hp2 == 0:
                    ctx.wait('t', 'V', vprod_ev[-1])
                ve_half = [0, 0]
                for n in range(4):
                    if n >= 2:
                        ctx.wait('t', 'V', ve_half[n % 2])
                    for dc in range(8):
                        top(lambda e, p_=psu, w_=wsb, n_=n, d_=dc, h_=hp2: e.matmul(
                            p_[:, (n_ % 2) * 512:((n_ % 2) + 1) * 512],
                            w_[:, d_, h_ * 128:(h_ + 1) * 128],
                            xn_fm[:, d_, n_ * 512:(n_ + 1) * 512],
                            start=(d_ == 0), stop=(d_ == 7)),
                            inc=(dc == 7))
                    ctx.wait('v', 'T', c['T'])
                    ve_half[n % 2] = vop(lambda e, p_=psu, d_=dsb, n_=n: e.tensor_copy(
                        d_[:, n_ * 512:(n_ + 1) * 512],
                        p_[:, (n_ % 2) * 512:((n_ % 2) + 1) * 512]))
                return c['V']

            for hp in range(NHP):
                pb = hp % 2
                if hp == 0:
                    qkprod(0, 0)
                    qk_ready_hp[0] = qkprod(0, 1)
                qk_ready = qk_ready_hp[hp]
                if STOP == 'C1':
                    continue

                for qc in range(4):
                    q0 = qc * 512
                    pv_args = {}

                    def emit_S_exp(up):
                        ge, go = 2 * up, 2 * up + 1
                        par = up % 2
                        sb = [PSA, PSB][par]
                        eb = et[par]
                        if up == 0:
                            ctx.wait('t', 'V', qk_ready)
                        ctx.wait('t', 'S', last_exp_bank[par])
                        top(lambda e, s_=sb: e.matmul(
                            s_[0:64, 0:512], k_sb[pb][0:64, ge * 64:(ge + 1) * 64],
                            q_sb[pb][0:64, q0:q0 + 512], start=True, stop=True,
                            skip_group_check=True), inc=False)
                        top(lambda e, s_=sb: e.matmul(
                            s_[64:128, 0:512], k_sb[pb][0:64, go * 64:(go + 1) * 64],
                            q_sb[pb][0:64, q0:q0 + 512], start=True, stop=True,
                            skip_group_check=True), inc=False)
                        top(lambda e, s_=sb: e.matmul(
                            s_[0:64, 512:1024], k_sb[pb][64:128, ge * 64:(ge + 1) * 64],
                            q_sb[pb][64:128, q0:q0 + 512], start=True, stop=True,
                            skip_group_check=True), inc=False)
                        tS = top(lambda e, s_=sb: e.matmul(
                            s_[64:128, 512:1024], k_sb[pb][64:128, go * 64:(go + 1) * 64],
                            q_sb[pb][64:128, q0:q0 + 512], start=True, stop=True,
                            skip_group_check=True))
                        ctx.wait('s', 'T', tS)
                        ctx.wait('s', 'T', last_pv_et[par])
                        se = sop(lambda e, s_=sb, b_=par: e.activation(
                            et[b_][:, :], s_[:, :], ACT.Exp, scale=0.125))
                        last_exp_bank[par] = se
                        pv_args[up] = (ge, go, par, se)
                        return se

                    def emit_PVl(up):
                        ge, go, par, se = pv_args[up]
                        eb = et[par]
                        ctx.wait('t', 'S', se)
                        if up == 0:
                            ctx.wait('t', 'V', ep_psum_free)
                        ste = (up == 0)
                        stp = (up == NG // 2 - 1)
                        vA_e = v_sb[0:64, ge // 2, hp * 128:hp * 128 + 64]
                        vB_e = v_sb[0:64, ge // 2, hp * 128 + 64:hp * 128 + 128]
                        vA_o = v_sb[64:128, go // 2, hp * 128:hp * 128 + 64]
                        vB_o = v_sb[64:128, go // 2, hp * 128 + 64:hp * 128 + 128]
                        top(lambda e, v_=vA_e, b_=eb: e.matmul(
                            PSO1[0:64, :], v_, b_[0:64, 0:512], start=ste, stop=stp,
                            skip_group_check=True), inc=False)
                        top(lambda e, v_=vB_e, b_=eb: e.matmul(
                            PSO1[64:128, :], v_, b_[0:64, 512:1024], start=ste, stop=stp,
                            skip_group_check=True), inc=False)
                        top(lambda e, b_=eb: e.matmul(
                            PSL1[0:64, :], ONE[64:128, 0:64], b_[64:128, 0:512],
                            start=ste, stop=stp, skip_group_check=True), inc=False)
                        top(lambda e, b_=eb: e.matmul(
                            PSL1[64:128, :], ONE[64:128, 0:64], b_[64:128, 512:1024],
                            start=ste, stop=stp, skip_group_check=True), inc=False)
                        top(lambda e, v_=vA_o, b_=eb: e.matmul(
                            PSO2[0:64, :], v_, b_[64:128, 0:512], start=ste, stop=stp,
                            skip_group_check=True), inc=False)
                        top(lambda e, v_=vB_o, b_=eb: e.matmul(
                            PSO2[64:128, :], v_, b_[64:128, 512:1024], start=ste, stop=stp,
                            skip_group_check=True), inc=False)
                        top(lambda e, b_=eb: e.matmul(
                            PSL2[0:64, :], ONE[0:64, 0:64], b_[0:64, 0:512],
                            start=ste, stop=stp, skip_group_check=True), inc=False)
                        tpv_ = top(lambda e, b_=eb: e.matmul(
                            PSL2[64:128, :], ONE[0:64, 0:64], b_[0:64, 512:1024],
                            start=ste, stop=stp, skip_group_check=True))
                        last_pv_et[par] = tpv_
                        return tpv_

                    for up in range(NG // 2):
                        emit_S_exp(up)
                        last_exp_all = pv_args[up][3]
                        if up == NG // 2 - 1 and qc == 3 and hp + 1 < NHP:
                            qkprod(hp + 1, 0)
                        if up >= 1:
                            emit_PVl(up - 1)
                    tpv = emit_PVl(NG // 2 - 1)
                    if qc == 3 and hp + 1 < NHP:
                        qk_ready_hp[hp + 1] = qkprod(hp + 1, 1)
                    if STOP in ('C2', 'C3'):
                        continue
                    ctx.wait('v', 'T', tpv)
                    vdrain()
                    vop(lambda e: e.tensor_copy(sgf[:, :], PSL1[:, :]))
                    vop(lambda e: e.tensor_copy(osum[:, :], PSO1[:, :]))
                    vdrain()
                    vop(lambda e: e.tensor_tensor(sgf[:, :], PSL2[:, :], sgf[:, :],
                                                  OP.add))
                    vop(lambda e: e.tensor_tensor(osum[:, :], PSO2[:, :], osum[:, :],
                                                  OP.add))
                    ep_psum_free = c['V']
                    vdrain()
                    vop(lambda e: e.reciprocal_approx_fast(lrec[:, :], sgf[:, :]))
                    vdrain()
                    ep_free_v = vop(lambda e, h_=hp, q_=q0: e.tensor_tensor(
                        o_sb[:, h_, q_:q_ + 512], osum[:, :], lrec[:, :], OP.mult))

            wo_l = dma_w(wo_sb[:, :, :], wob.rearrange("(a p) n -> p a n", p=128))
            ctx.wait('t', 'W', wo_l)
            ctx.wait('t', 'V', ep_free_v)
            ctx.wait('t', 'S', last_exp_all)
            xq_evac = []
            xmid_store = {}
            vxm_hist = {}
            trD_T = {}
            evD = {}
            qD = {}
            ctx.wait('s', 'T', c['T'])
            ctx.on('s', lambda e: e.dma_start(
                out=wg_sb[:, 0:2, :],
                in_=wgqb.rearrange("(a p) n -> p a n", p=128)[:, 0:2, :]),
                inc='C', n=16)
            ctx.on('s', lambda e: e.dma_start(
                out=wu_sb[:, 5:8, :],
                in_=wuqb.rearrange("(a p) n -> p a n", p=128)[:, 5:8, :]),
                inc='C', n=16)

            def emit_oprojD(t_):
                psu = [PSA, PSB][t_ % 2]
                if t_ >= 2:
                    ctx.wait('t', 'V', vxm_hist[t_ - 2])
                for n in range(2):
                    for ic in range(8):
                        top(lambda e, p_=psu, n_=n, i_=ic, tt_=t_: e.matmul(
                            p_[:, n_ * 512:(n_ + 1) * 512],
                            o_sb[:, i_, tt_ * 128:(tt_ + 1) * 128],
                            wo_sb[:, i_, n_ * 512:(n_ + 1) * 512],
                            start=(i_ == 0), stop=False), inc=False)
                    top(lambda e, p_=psu, n_=n: e.matmul(
                        p_[:, n_ * 512:(n_ + 1) * 512], ONE[0:1, 0:128],
                        BOB[0:1, n_ * 512:(n_ + 1) * 512],
                        start=False, stop=True), inc=(n == 1))
                return c['T']

            def emit_trD(t_):
                ctx.wait('t', 'V', qD[t_].c_mul)
                if t_ >= 1:
                    ctx.wait('t', 'S', evD[t_ - 1])
                for j in range(8):
                    psl = [PSL1, PSL2][j // 4]
                    ttranspose(psl[:, (j % 4) * 128:((j % 4) + 1) * 128],
                               xf[t_ % 2][:, j * 128:(j + 1) * 128], inc=(j == 7))
                trD_T[t_] = c['T']

            def emit_evacD(t_):
                ctx.wait('s', 'T', trD_T[t_])
                ctx.on('s', lambda e, t__=t_: e.activation(
                    xq_fm[:, 0:4, t__ * 128:(t__ + 1) * 128],
                    PSL1[:, :].rearrange("p (a n) -> p a n", n=128), ACT.Copy),
                    inc=None, n=1)
                ctx.on('s', lambda e, t__=t_: e.activation(
                    xq_fm[:, 4:8, t__ * 128:(t__ + 1) * 128],
                    PSL2[:, :].rearrange("p (a n) -> p a n", n=128), ACT.Copy),
                    inc='S', n=1)
                evD[t_] = c['S']

            oproj_T_last = 0
            for t in range(NT):
                b = t % 2
                psu = [PSA, PSB][t % 2]
                tD = emit_oprojD(t)
                oproj_T_last = tD
                g_eng, g_val = xf2_guard[b]
                if g_eng == 'G':
                    ctx.waitd('y', 'G')
                else:
                    ctx.wait('y', g_eng, g_val)
                dma_y(xf2[b][:, :], x_in[xbase + t * 128: xbase + (t + 1) * 128, :],
                      sem='M%d' % b)
                ctx.wait('v', 'T', tD)
                ctx.waitd('v', 'M%d' % b)
                vxm = vop(lambda e, p_=psu, bb=b: e.tensor_tensor(
                    xf2[bb][:, :], p_[:, :], xf2[bb][:, :], OP.add))
                vxm_hist[t] = vxm
                ctx.wait('g', 'V', vxm)
                xmid_store[t] = dma_g(xmid_hbm[t * 128:(t + 1) * 128, :], xf2[b][:, :])
                xf2_guard[b] = ('G', xmid_store[t])
                ctx.wait('v', 'V', vxm)
                vop(lambda e, bb=b: e.bn_stats(bnb[:, 0:6], xf2[bb][:, 0:512]), inc=False)
                vbn1 = vop(lambda e, bb=b: e.bn_stats(bnb[:, 6:12], xf2[bb][:, 512:1024]))
                if t >= 1:
                    qD[t - 1].s_add()
                ctx.wait('v', 'V', vbn1)
                vbn2 = vop(lambda e: e.bn_aggr(mv[:, 0:2], bnb[:, 0:12]))
                if t >= 1:
                    qD[t - 1].s_mult(dst_war=trD_T.get(t - 3, 0))
                    emit_trD(t - 1)
                ctx.wait('s', 'V', vbn2)
                sop(lambda e: e.activation(lnv[:, 0:1], mv[:, 1:2], ACT.Ln,
                                           bias=epsb[:, 0:1]))
                ctx.wait('s', 'S', c['S'])
                srs2 = sop(lambda e: e.activation(rstd[:, 0:1], lnv[:, 0:1], ACT.Exp,
                                                  scale=-0.5))
                if t >= 1:
                    emit_evacD(t - 1)
                ctx.wait('v', 'S', srs2)
                if t >= 1:
                    ctx.wait('v', 'V', qD[t - 1].c_ops)
                vap2 = vop(lambda e, bb=b: e.tensor_scalar(
                    xn2f[:, :], xf2[bb][:, :], mv[:, 0:1], rstd[:, 0:1],
                    OP.subtract, OP.mult))
                g_eng, g_val = xf_guard[b]
                qw = qD[t - 2].c_mul if t >= 2 else 0
                q = Quant(xf[b], xn2f, 1024, war=qw)
                if g_eng == 'V':
                    pass
                q.s_reduce(src_done=vap2)
                q.s_scale()
                q.s_recip()
                q.s_ops()
                qD[t] = q
            qD[NT - 1].s_add()
            qD[NT - 1].s_mult(dst_war=trD_T.get(NT - 3, 0))
            emit_trD(NT - 1)
            emit_evacD(NT - 1)
            for b in range(2):
                xf_guard[b] = ('T', trD_T[NT - 1 if (NT - 1) % 2 == b else NT - 2])

            wd_l = dma_w(wd_sb[:, :, :], wdqb.rearrange("(a p) n -> p a n", p=128))
            ctx.wait('s', 'T', oproj_T_last)
            wg_l = dma_w(wg_sb[:, 2:8, :], wgqb.rearrange("(a p) n -> p a n", p=128)[:, 2:8, :])
            wu_l = dma_w(wu_sb[:, 0:5, :], wuqb.rearrange("(a p) n -> p a n", p=128)[:, 0:5, :])

            ctx.wait('t', 'W', wu_l)
            ctx.waitd('t', 'C')
            PAIR_W = [1024, 1024, 768]
            IC_W = [512, 512, 512, 512, 512, 256]
            NPAIR = NT * 3
            qF = {}
            hm1 = {}
            trF = {}
            evF = {}
            dnF = {}
            vo2_hist = {}

            def emit_gateupF(pi):
                t_, pc_ = divmod(pi, 3)
                ctx.wait('t', 'S', evD[t_])
                ctx.wait('t', 'V', hm1.get(pi - 1, 0))
                res = []
                for half in (0, 1):
                    ic = 2 * pc_ + half
                    ww = IC_W[ic]
                    psu = [PSA, PSB][half]
                    for wsb in (wg_sb, wu_sb):
                        for hc in range(8):
                            top(lambda e, p_=psu, c_=hc, w_=wsb, i_=ic, ww_=ww,
                                g_=(0 if wsb is wg_sb else 1), tt_=t_: e.matmul(
                                    p_[:, g_ * 512:g_ * 512 + ww_],
                                    xq_fm[:, c_, tt_ * 128:(tt_ + 1) * 128],
                                    w_[:, c_, i_ * 512:i_ * 512 + ww_],
                                    start=(c_ == 0), stop=(c_ == 7)),
                                inc=(hc == 7))
                    res.append(c['T'])
                return res

            def emit_hmults(pi, tgu):
                t_, pc_ = divmod(pi, 3)
                base = 0
                last = 0
                if pi == 0:
                    ctx.wait('s', 'V', ep_free_v)
                for half in (0, 1):
                    ic = 2 * pc_ + half
                    ww = IC_W[ic]
                    psu = [PSA, PSB][half]
                    ctx.wait('s', 'T', tgu[half])
                    if half == 1:
                        ctx.wait('s', 'V', c['V'])
                    ssg = sop(lambda e, p_=psu, ww_=ww: e.activation(
                        sgf[:, 0:ww_], p_[:, 0:ww_], ACT.Silu))
                    ctx.wait('v', 'S', ssg)
                    if half == 0 and pi >= 1:
                        ctx.wait('v', 'V', qF[pi - 1].c_ops)
                    last = vop(lambda e, p_=psu, ww_=ww, b_=base: e.tensor_tensor(
                        xn2f[:, b_:b_ + ww_], sgf[:, 0:ww_], p_[:, 512:512 + ww_],
                        OP.mult))
                    base += ww
                hm1[pi] = last
                return last

            def emit_evacF(pi):
                t_, pc_ = divmod(pi, 3)
                nsub = PAIR_W[pc_] // 128
                ctx.wait('v', 'T', trF[pi])
                ctx.wait('v', 'T', dnF.get(pi - 1, 0))
                n1 = min(nsub, 4)
                vop(lambda e, n_=n1: e.tensor_copy(hq_fm[:, 0:n_ * 128],
                                                   PSL1[:, 0:n_ * 128]),
                    inc=(nsub <= 4))
                if nsub > 4:
                    vop(lambda e, n_=nsub - 4: e.tensor_copy(
                        hq_fm[:, 512:512 + n_ * 128], PSL2[:, 0:n_ * 128]))
                evF[pi] = c['V']

            def emit_trF(pi):
                t_, pc_ = divmod(pi, 3)
                nsub = PAIR_W[pc_] // 128
                ctx.wait('t', 'V', qF[pi].c_mul)
                if pi == 0:
                    ctx.wait('t', 'S', evD[NT - 1])
                ctx.wait('t', 'V', evF.get(pi - 1, 0))
                for j in range(nsub):
                    psl = [PSL1, PSL2][j // 4]
                    ttranspose(psl[:, (j % 4) * 128:((j % 4) + 1) * 128],
                               xf2[pi % 2][:, j * 128:(j + 1) * 128],
                               inc=(j == nsub - 1))
                trF[pi] = c['T']

            def emit_downF(pi):
                t_, pc_ = divmod(pi, 3)
                nsub = PAIR_W[pc_] // 128
                ctx.wait('t', 'V', evF[pi])
                if pc_ == 0:
                    ctx.wait('t', 'V', vo2_hist.get(t_ - 1, 0))
                for j in range(nsub):
                    isub = 8 * pc_ + j
                    st_ = (isub == 0)
                    sp_ = (isub == 21)
                    top(lambda e, j_=j, i_=isub, st__=st_, sp__=sp_: e.matmul(
                        PSO1[:, :], hq_fm[:, j_ * 128:(j_ + 1) * 128],
                        wd_sb[:, i_, 0:512], start=st__, stop=sp__), inc=False)
                    top(lambda e, j_=j, i_=isub, st__=st_, sp__=sp_: e.matmul(
                        PSO2[:, :], hq_fm[:, j_ * 128:(j_ + 1) * 128],
                        wd_sb[:, i_, 512:1024], start=st__, stop=sp__),
                        inc=(j == nsub - 1))
                dnF[pi] = c['T']

            def emit_residF_load(t_):
                b = t_ % 2
                g_eng, g_val = xf_guard[b]
                if g_eng == 'G':
                    ctx.waitd('y', 'G')
                else:
                    ctx.wait('y', g_eng, g_val)
                ctx.waitd('y', 'G')
                dma_y(xf[b][:, :], xmid_hbm[t_ * 128:(t_ + 1) * 128, :],
                      sem='L%d' % b)

            def emit_residF(t_):
                b = t_ % 2
                tdn = dnF[3 * t_ + 2]
                ctx.wait('v', 'T', tdn)
                ctx.waitd('v', 'L%d' % b)
                vop(lambda e, b_=b: e.tensor_tensor(
                    xf[b_][:, 0:512], PSO1[:, :], xf[b_][:, 0:512], OP.add))
                vo2 = vop(lambda e, b_=b: e.tensor_tensor(
                    xf[b_][:, 512:1024], PSO2[:, :], xf[b_][:, 512:1024], OP.add))
                vo2_hist[t_] = vo2
                ctx.wait('g', 'V', vo2)
                gst = dma_g(y_out[xbase + t_ * 128: xbase + (t_ + 1) * 128, :],
                            xf[b][:, :])
                xf_guard[b] = ('G', gst)

            for pi in range(NPAIR):
                t_, pc_ = divmod(pi, 3)
                tgu = emit_gateupF(pi)
                if pi >= 1:
                    qF[pi - 1].s_add()
                    if pi <= 2:
                        ctx.waitd('v', 'G')
                    qF[pi - 1].s_mult(dst_war=trF.get(pi - 3, 0))
                    emit_trF(pi - 1)
                if pc_ == 0 and t_ >= 1:
                    emit_residF_load(t_ - 1)
                emit_hmults(pi, tgu)
                q = Quant(xf2[pi % 2], xn2f, PAIR_W[pc_])
                qF[pi] = q
                if pi >= 1:
                    emit_evacF(pi - 1)
                    if (pi - 1) % 3 == 0 and t_ >= 1:
                        emit_residF(t_ - 1)
                    emit_downF(pi - 1)
                q.s_reduce(src_done=hm1[pi])
                q.s_scale()
                q.s_recip()
                q.s_ops()
            pi = NPAIR - 1
            qF[pi].s_add()
            qF[pi].s_mult(dst_war=trF.get(pi - 2, 0))
            emit_trF(pi)
            emit_residF_load(NT - 1)
            emit_evacF(pi)
            emit_downF(pi)
            emit_residF(NT - 1)
            for b in range(2):
                last_pi = NPAIR - 1 if (NPAIR - 1) % 2 == b else NPAIR - 2
                xf2_guard[b] = ('T', trF[last_pi])
            prev_seq_V_end = c['V']



        ctx.on('g', lambda e: e.wait_ge(ctx.sem['G'], c['G']))

    with (
        nc.semaphore("semL0") as semL0,
        nc.semaphore("semL1") as semL1,
        nc.semaphore("semM0") as semM0,
        nc.semaphore("semM1") as semM1,
        nc.semaphore("semC") as semC,
        nc.semaphore("semW") as semW,
        nc.semaphore("semG") as semG,
        nc.semaphore("semT") as semT,
        nc.semaphore("semV") as semV,
        nc.semaphore("semS") as semS,
        nc.Block() as block,
    ):
        sems = {"L0": semL0, "L1": semL1, "M0": semM0, "M1": semM1, "C": semC,
                "W": semW, "G": semG, "T": semT, "V": semV, "S": semS}

        @block.sync
        def _(eng):
            emit(Ctx('y', eng, sems))

        @block.vector
        def _(eng):
            emit(Ctx('v', eng, sems))

        @block.scalar
        def _(eng):
            emit(Ctx('s', eng, sems))

        @block.tensor
        def _(eng):
            emit(Ctx('t', eng, sems))

        @block.gpsimd
        def _(eng):
            emit(Ctx('g', eng, sems))

    mybir.codegen_inst_isa_subclasses(nc)
    return nc


_NC_CACHE = {}


def kernel(L0_attn_hidden_states, L0_attn_Wq, L0_attn_Wkv, L0_attn_Wo, L0_attn_bo,
           L0_ffn_w_gate, L0_ffn_w_up, L0_ffn_w_down):
    x = np.ascontiguousarray(np.asarray(L0_attn_hidden_states, dtype=np.float32))
    B = x.shape[0]
    wq = np.ascontiguousarray(np.asarray(L0_attn_Wq, dtype=np.float32))
    wkv = np.asarray(L0_attn_Wkv, dtype=np.float32)
    wk = np.ascontiguousarray(wkv[:, :D])
    wv = np.ascontiguousarray(wkv[:, D:])
    wo = np.ascontiguousarray(np.asarray(L0_attn_Wo, dtype=np.float32))
    bo = np.ascontiguousarray(np.asarray(L0_attn_bo, dtype=np.float32)).reshape(1, D)
    wg = np.ascontiguousarray(np.asarray(L0_ffn_w_gate, dtype=np.float32))
    wu = np.ascontiguousarray(np.asarray(L0_ffn_w_up, dtype=np.float32))
    wd = np.ascontiguousarray(np.asarray(L0_ffn_w_down, dtype=np.float32))
    ident = np.eye(128, dtype=np.float32)
    onesb = np.ones((128, 128), dtype=ml_dtypes.bfloat16)

    if "nc" not in _NC_CACHE:
        _NC_CACHE["nc"] = build_program()
    nc = _NC_CACHE["nc"]

    n_cores = 8
    per = B // n_cores
    in_maps = []
    for ci in range(n_cores):
        shard = np.ascontiguousarray(x[ci * per:(ci + 1) * per].reshape(per * S, D))
        in_maps.append({
            "x": shard, "wq": wq, "wk": wk, "wv": wv, "wo": wo, "bo": bo,
            "wg": wg, "wu": wu, "wd": wd, "ident": ident, "onesb": onesb,
        })
    res = run_bass_kernel_spmd(nc, in_maps, core_ids=list(range(n_cores)))
    return np.concatenate([r["y"].reshape(per, S, D) for r in res.results], axis=0)
